# version 03
# baseline (speedup 1.0000x reference)
import numpy as np
import ml_dtypes
import concourse.bass as bass
import concourse.mybir as mybir
from concourse.bass_utils import run_bass_kernel_spmd
from concourse.alu_op_type import AluOpType
from contextlib import ExitStack

F32 = mybir.dt.float32
BF16 = mybir.dt.bfloat16
AF = mybir.ActivationFunctionType
AX = mybir.AxisListType

D = 2048
T = 4096
NF = 16
HID = 5632
NH = 44
H = 16
TB = 1024
NTB_ALL = 4
NTB_OWN = 2
SCALE = 1.0 / np.sqrt(np.float32(128.0))
EPS_LN = 1e-5
EPS_Q = 1e-8


class Emit:
    def __init__(self):
        self.ops = {k: [] for k in ("sy", "gp", "te", "ac", "ve")}
        self.cnt = {k: 0 for k in ("MM", "AC", "VE")}

    def inc(self, sem, n=1):
        self.cnt[sem] += n
        return self.cnt[sem]

    def on(self, eng):
        def deco(fn):
            self.ops[eng].append(fn)
        return deco

    def snap(self):
        return dict(self.cnt)


_UPTO = "full"


def build_program():
    upto = _UPTO
    nc = bass.Bass("TRN2", target_bir_lowering=False, debug=False)

    x0 = nc.dram_tensor("x0", [D, T], F32, kind="ExternalInput").ap()
    w1T = nc.dram_tensor("w1T", [D, HID], F32, kind="ExternalInput").ap()
    w2T = nc.dram_tensor("w2T", [D, HID], F32, kind="ExternalInput").ap()
    w3T = nc.dram_tensor("w3T", [HID, D], F32, kind="ExternalInput").ap()
    wqkvT = nc.dram_tensor("wqkvT", [D, 3 * D], F32, kind="ExternalInput").ap()
    woT = nc.dram_tensor("woT", [D, D], F32, kind="ExternalInput").ap()
    biasq = nc.dram_tensor("biasq", [128, 48], F32, kind="ExternalInput").ap()
    biaso = nc.dram_tensor("biaso", [128, 16], F32, kind="ExternalInput").ap()
    cosb = nc.dram_tensor("cosb", [128, T], BF16, kind="ExternalInput").ap()
    sinb = nc.dram_tensor("sinb", [128, T], BF16, kind="ExternalInput").ap()
    trim = nc.dram_tensor("trim", [128, 128], BF16, kind="ExternalInput").ap()
    b0m = nc.dram_tensor("b0m", [128, 128], BF16, kind="ExternalInput").ap()
    idn_in = nc.dram_tensor("idn_in", [128, 128], BF16, kind="ExternalInput").ap()
    rot_in = nc.dram_tensor("rot_in", [128, 128], BF16, kind="ExternalInput").ap()
    out = nc.dram_tensor("out", [D, D], F32, kind="ExternalOutput").ap()
    if upto == "f1":
        dbg1 = nc.dram_tensor("dbg1", [D, T], F32, kind="ExternalOutput").ap()
    elif upto == "q":
        dbg1 = nc.dram_tensor("dbg1", [D, HID], BF16, kind="ExternalOutput").ap()
        dbg2 = nc.dram_tensor("dbg2", [D, HID], BF16, kind="ExternalOutput").ap()
        dbg3 = nc.dram_tensor("dbg3", [HID, D], BF16, kind="ExternalOutput").ap()
        dbg4 = nc.dram_tensor("dbg4", [128, NH + 2], F32, kind="ExternalOutput").ap()
    elif upto == "a":
        dbg1 = nc.dram_tensor("dbg1", [D, T], BF16, kind="ExternalOutput").ap()
        dbg2 = nc.dram_tensor("dbg2", [D, T], BF16, kind="ExternalOutput").ap()
        dbg3 = nc.dram_tensor("dbg3", [D, D], BF16, kind="ExternalOutput").ap()
    elif upto == "w":
        dbg1 = nc.dram_tensor("dbg1", [D, D], F32, kind="ExternalOutput").ap()
    elif upto == "t":
        dbg1 = nc.dram_tensor("dbg1", [128, H * D], BF16, kind="ExternalOutput").ap()

    w12S = nc.dram_tensor("w12S", [NH * 128, 2 * D], BF16).ap()
    w3S = nc.dram_tensor("w3S", [NF * 128, HID], BF16).ap()
    wqkvS = nc.dram_tensor("wqkvS", [48 * 128, D], BF16).ap()
    woS = nc.dram_tensor("woS", [NF * 128, D], BF16).ap()
    xA = nc.dram_tensor("xA", [D, T], F32).ap()
    xB = nc.dram_tensor("xB", [D, D], F32).ap()
    qs = nc.dram_tensor("qs", [D, D], BF16).ap()
    ks = nc.dram_tensor("ks", [D, T], BF16).ap()
    vs = nc.dram_tensor("vs", [D, T], BF16).ap()

    E = Emit()

    with ExitStack() as ctx:
        en = ctx.enter_context
        AR_HID = en(nc.sbuf_tensor("AR_HID", [128, 45056], BF16))
        AR_U = en(nc.sbuf_tensor("AR_U", [128, 16384], BF16))
        AR_W = en(nc.sbuf_tensor("AR_W", [128, 11264], BF16))
        onescol_bf = en(nc.sbuf_tensor("onescol_bf", [128, 1], BF16))
        onescol_f = en(nc.sbuf_tensor("onescol_f", [128, 1], F32))
        onesrow_f = en(nc.sbuf_tensor("onesrow_f", [1, 128], F32))
        epsrow = en(nc.sbuf_tensor("epsrow", [1, 1], F32))
        tri_t = en(nc.sbuf_tensor("tri_t", [128, 128], BF16))
        b0_t = en(nc.sbuf_tensor("b0_t", [128, 128], BF16))
        zer_t = en(nc.sbuf_tensor("zer_t", [128, 128], BF16))
        idn_t = en(nc.sbuf_tensor("idn_t", [128, 128], BF16))
        rot_t = en(nc.sbuf_tensor("rot_t", [128, 128], BF16))
        bq_t = en(nc.sbuf_tensor("bq_t", [128, 48], F32))
        bo_t = en(nc.sbuf_tensor("bo_t", [128, 16], F32))
        mrow = en(nc.sbuf_tensor("mrow", [1, TB], F32))
        msqrow = en(nc.sbuf_tensor("msqrow", [1, TB], F32))
        rrow = en(nc.sbuf_tensor("rrow", [1, TB], F32))
        mrrow = en(nc.sbuf_tensor("mrrow", [1, TB], F32))
        recrow = en(nc.sbuf_tensor("recrow", [1, 512], F32))
        throw = en(nc.sbuf_tensor("throw", [1, 1], F32))
        thcol = en(nc.sbuf_tensor("thcol", [128, 1], F32))
        nthcol = en(nc.sbuf_tensor("nthcol", [128, 1], F32))
        gparts = en(nc.sbuf_tensor("gparts", [128, NH], F32))
        gsum = en(nc.sbuf_tensor("gsum", [128, 1], F32))
        gdbg = en(nc.sbuf_tensor("gdbg", [128, NH + 2], F32)) if upto == "q" else None
        rbb = en(nc.sbuf_tensor("rbb", [128, TB], F32))
        mrbb = en(nc.sbuf_tensor("mrbb", [128, TB], F32))
        xsq = en(nc.sbuf_tensor("xsq", [128, 2 * TB], BF16))
        xfs = en(nc.sbuf_tensor("xfs", [128, TB], F32))
        t1f = en(nc.sbuf_tensor("t1f", [128, TB], F32))
        sigb = en(nc.sbuf_tensor("sigb", [128, 2 * TB], BF16))
        x2b = en(nc.sbuf_tensor("x2b", [128, 2 * TB], BF16))
        xres = en(nc.sbuf_tensor("xres", [128, 2 * TB], F32))
        xbout = en(nc.sbuf_tensor("xbout", [128, 2 * TB], F32))

        PSF = en(nc.psum_tensor("PSF", [128, 4096], F32))
        PSTR = PSF[:, 3584:4096].bitcast(BF16)

        sMM = en(nc.semaphore("sMM"))
        sAC = en(nc.semaphore("sAC"))
        sVE = en(nc.semaphore("sVE"))
        SEM = {"MM": sMM, "AC": sAC, "VE": sVE}

        def sem_for(key):
            if key not in SEM:
                SEM[key] = en(nc.semaphore("s_" + key))
                E.cnt[key] = 0
            return SEM[key]

        hid_v = AR_HID[:, 0:NH * TB]
        xbf_F = AR_HID[:, 28 * TB * 1:44 * TB]
        wtileA = AR_HID[:, 0:11264].bitcast(F32)
        wtileB = AR_HID[:, 11264:22528].bitcast(F32)
        tmpa_q = AR_HID[:, 22528:28160]
        tmpb_q = AR_HID[:, 28160:33792]
        wqtileA = AR_HID[:, 33792:39424]
        wqtileB = AR_HID[:, 39424:45056]
        xbf_A = AR_HID[:, 0:16 * TB]
        cos_t = AR_HID[:, 16 * TB:17 * TB]
        sin_t = AR_HID[:, 17 * TB:18 * TB]
        o_sb = AR_HID[:, 0:H * D]
        vtm_v = AR_HID[:, H * D:H * D + 4096]
        expS0 = AR_HID[:, H * D + 4096:H * D + 5120]
        expS1 = AR_HID[:, H * D + 5120:H * D + 6144]
        u_v = AR_U[:, 0:16 * TB]
        kh_v = [AR_U[:, 0:4096], AR_U[:, 8192:12288]]
        vh_v = [AR_U[:, 4096:8192], AR_U[:, 12288:16384]]
        castA = AR_U[:, 0:6144]
        castB = AR_U[:, 6144:12288]
        w12s = [AR_W[:, 0:4096], AR_W[:, 4096:8192]]
        w3s = [AR_W[:, 0:5632], AR_W[:, 5632:11264]]
        wqs = [AR_W[:, 0:2048], AR_W[:, 2048:4096]]
        wos = [AR_W[:, 0:2048], AR_W[:, 2048:4096]]
        qkbf = AR_W[:, 4096:6144]
        qkout = AR_W[:, 6144:8192]
        rtmp1 = AR_W[:, 8192:9216]
        rtmp2 = AR_W[:, 9216:10240]
        qh_t = AR_W[:, 4096:6144]
        oun_t = AR_W[:, 6144:6656]
        recb_t = AR_W[:, 6656:7168]

        def dma(q, key, out_ap, in_ap):
            sem = sem_for(key)
            v = E.inc(key, 16)

            @E.on(q)
            def _(e, out_ap=out_ap, in_ap=in_ap, sem=sem):
                e.dma_start(out=out_ap, in_=in_ap).then_inc(sem, 16)
            return v

        def wait(eng, sem_key, val):
            if val is None or val <= 0:
                return
            sem = sem_for(sem_key)

            @E.on(eng)
            def _(e, s=sem, v=val):
                e.wait_ge(s, v)

        def chunk_cols(src2d, c0, w):
            return src2d[:, c0:c0 + w].rearrange("(a p) c -> p a c", p=128)

        def row_tile(src2d, t, C):
            return src2d[t * 128:(t + 1) * 128, :].rearrange("(a p) c -> p a c", p=128)

        def sb3(view, w):
            return view.rearrange("p (a c) -> p a c", c=w)

        def one3(view, w):
            return view[:, 0:w].rearrange("p (a c) -> p a c", c=w)

        dma("sy", "ini", tri_t[:], trim)
        dma("sy", "ini", b0_t[:], b0m)
        dma("sy", "ini", idn_t[:], idn_in)
        dma("sy", "ini", rot_t[:], rot_in)
        dma("sy", "ini", bq_t[:], biasq)
        c_bo = dma("sy", "ini", bo_t[:], biaso)
        wait("ve", "ini", c_bo)

        @E.on("ve")
        def _(v):
            v.memset(onescol_bf[:], 1.0)
            v.memset(onescol_f[:], 1.0)
            v.memset(onesrow_f[:], 1.0)
            v.memset(epsrow[:], EPS_LN)
            v.memset(zer_t[:], 0.0)
            v.memset(gparts[:], 0.0).then_inc(sVE, 1)
        vInit = E.inc("VE")

        prev_ac = 0
        prev_ve = vInit
        def qdest(which, t):
            if which == 0:
                return w12S[:, t * 128:(t + 1) * 128].rearrange(
                    "(hc p) c -> p hc c", p=128)
            if which == 1:
                return w12S[:, 2048 + t * 128:2048 + (t + 1) * 128].rearrange(
                    "(hc p) c -> p hc c", p=128)
            return w3S[:, t * 128:(t + 1) * 128].rearrange(
                "(fo p) c -> p fo c", p=128)

        for wi, (wsrc, ntile, CW) in enumerate(((w1T, NF, HID), (w2T, NF, HID),
                                                (w3T, NH, D))):
            tile_ve = []
            for t in range(ntile):
                buf = wtileA if t % 2 == 0 else wtileB
                gk = "qg0" if t % 2 == 0 else "qg1"
                wait("sy", "VE", tile_ve[t - 2] if t >= 2 else prev_ve)
                c_ld = dma("sy", gk, one3(buf, CW), row_tile(wsrc, t, CW))
                wait("ve", gk, c_ld)

                @E.on("ve")
                def _(v, buf=buf, t=t, CW=CW):
                    v.reduce_sum(gparts[:, t:t + 1], buf[:, 0:CW], axis=AX.X,
                                 apply_absolute_value=True).then_inc(sVE, 1)
                tile_ve.append(E.inc("VE"))

            @E.on("ve")
            def _(v, nt=ntile, w_=tile_ve[-1]):
                v.wait_ge(sVE, w_)
                v.reduce_sum(gsum[:, 0:1], gparts[:, 0:nt], axis=AX.X).then_inc(sVE, 1)
            vred = E.inc("VE")
            if upto == "q" and wsrc is w1T:
                @E.on("ve")
                def _(v):
                    v.tensor_copy(gdbg[:, 0:NH], gparts[:])
                    v.tensor_copy(gdbg[:, NH:NH + 1], gsum[:]).then_inc(sVE, 1)
                E.inc("VE")
            wait("te", "VE", vred)
            wait("te", "AC", prev_ac)

            @E.on("te")
            def _(te):
                te.matmul(PSF[0:1, 0:1], gsum[:, 0:1], onescol_f[:, 0:1],
                          start=True, stop=True).then_inc(sMM, 1)
            mm_g = E.inc("MM")
            wait("ve", "MM", mm_g)

            @E.on("ve")
            def _(v, scale=0.5 / (128.0 * ntile * CW)):
                v.tensor_scalar(throw[0:1, 0:1], PSF[0:1, 0:1], scale, 0.5 * EPS_Q,
                                AluOpType.mult, AluOpType.add).then_inc(sVE, 1)
            vth = E.inc("VE")
            wait("te", "VE", vth)

            @E.on("te")
            def _(te):
                te.matmul(PSF[:, 512:513], onesrow_f[0:1, :], throw[0:1, 0:1],
                          start=True, stop=True).then_inc(sMM, 1)
            mm_th = E.inc("MM")
            wait("ac", "MM", mm_th)

            @E.on("ac")
            def _(s):
                s.copy(thcol[:], PSF[:, 512:513]).then_inc(sAC, 1)
            prev_ac = E.inc("AC")
            wait("ve", "AC", prev_ac)

            @E.on("ve")
            def _(v):
                v.tensor_scalar(nthcol[:], thcol[:], -1.0, None,
                                AluOpType.mult).then_inc(sVE, 1)
            vneg = E.inc("VE")

            q_ve = []
            q_st = []
            for t in range(ntile):
                buf = wtileA if t % 2 == 0 else wtileB
                qbuf = wqtileA if t % 2 == 0 else wqtileB
                qk_ = "qq0" if t % 2 == 0 else "qq1"
                sk_ = "qs0" if t % 2 == 0 else "qs1"
                wait("sy", "VE", q_ve[t - 2] if t >= 2 else vneg)
                c_ld = dma("sy", qk_, one3(buf, CW), row_tile(wsrc, t, CW))
                wait("ve", qk_, c_ld)
                if t >= 2:
                    wait("ve", sk_, q_st[t - 2])

                @E.on("ve")
                def _(v, buf=buf, qbuf=qbuf, CW=CW):
                    v.tensor_scalar(tmpa_q[:, 0:CW], buf[:, 0:CW], thcol[:, 0:1], None,
                                    AluOpType.is_gt)
                    v.tensor_scalar(tmpb_q[:, 0:CW], buf[:, 0:CW], nthcol[:, 0:1], None,
                                    AluOpType.is_lt)
                    v.tensor_tensor(qbuf[:, 0:CW], tmpa_q[:, 0:CW], tmpb_q[:, 0:CW],
                                    AluOpType.subtract).then_inc(sVE, 1)
                q_ve.append(E.inc("VE"))
                wait("sy", "VE", q_ve[-1])
                q_st.append(dma("sy", sk_, qdest(wi, t),
                                qbuf[:, 0:CW].rearrange("p (a c) -> p a c", c=128)))
            prev_ve = q_ve[-1]

        cast_st = []
        for (wsrc, wdstS, CW, ntile) in ((wqkvT, wqkvS, 3 * D, NF), (woT, woS, D, NF)):
            for t in range(ntile):
                i2 = len(cast_st)
                lk = "cl0" if i2 % 2 == 0 else "cl1"
                sk = "cs0" if i2 % 2 == 0 else "cs1"
                buf = castA if i2 % 2 == 0 else castB
                if i2 >= 2:
                    wait("gp", sk, cast_st[-2][1])
                c_ld = dma("gp", lk, one3(buf, CW), row_tile(wsrc, t, CW))
                wait("sy", lk, c_ld)
                dstap = wdstS[:, t * 128:(t + 1) * 128].rearrange(
                    "(oc p) c -> p oc c", p=128)
                cast_st.append((sk, dma("sy", sk, dstap,
                                        buf[:, 0:CW].rearrange("p (a c) -> p a c", c=128))))

        fence_q = E.snap()

        def ln_prologue(xbf, src_dram, tsl, xbf_guard_mm, region_ac, region_ve,
                        raw_waits, prev_ve_fence):
            wait("gp", "MM", xbf_guard_mm)
            for (k_, v_) in raw_waits:
                wait("gp", k_, v_)
            wait("gp", "VE", prev_ve_fence)
            c_x = dma("gp", "gpx", sb3(xbf, TB), src_dram[:, tsl].rearrange("(a p) t -> p a t", p=128))
            wait("te", "gpx", c_x)
            wait("te", "AC", region_ac)
            wait("te", "VE", region_ve)
            @E.on("te")
            def _(te, xbf=xbf):
                for fc in range(NF):
                    for qb in range(2):
                        i = te.matmul(PSF[0:1, qb * 512:(qb + 1) * 512],
                                      onescol_bf[:, 0:1],
                                      xbf[:, fc * TB + qb * 512:fc * TB + qb * 512 + 512],
                                      start=(fc == 0), stop=(fc == NF - 1))
                i.then_inc(sMM, 1)
            E.inc("MM")
            sq_acs = []
            sq_mms = []
            for fc in range(NF):
                sqb = xsq[:, (fc % 2) * TB:(fc % 2 + 1) * TB]
                wait("ac", "gpx", c_x)
                if fc >= 2:
                    wait("ac", "MM", sq_mms[fc - 2])
                if fc == 0:
                    wait("ac", "MM", xbf_guard_mm)

                @E.on("ac")
                def _(s, sqb=sqb, fc=fc, xbf=xbf):
                    s.activation(sqb, xbf[:, fc * TB:(fc + 1) * TB],
                                 AF.Square).then_inc(sAC, 1)
                sq_acs.append(E.inc("AC"))
                wait("te", "AC", sq_acs[-1])

                @E.on("te")
                def _(te, fc=fc, sqb=sqb):
                    for qb in range(2):
                        i = te.matmul(PSF[0:1, 1024 + qb * 512:1536 + qb * 512],
                                      onescol_bf[:, 0:1], sqb[:, qb * 512:qb * 512 + 512],
                                      start=(fc == 0), stop=(fc == NF - 1))
                    i.then_inc(sMM, 1)
                sq_mms.append(E.inc("MM"))
            mm_stats = sq_mms[-1]
            wait("ve", "MM", mm_stats)

            @E.on("ve")
            def _(v):
                v.tensor_scalar(mrow[0:1, :], PSF[0:1, 0:TB], 1.0 / D, None,
                                AluOpType.mult)
                v.tensor_scalar(msqrow[0:1, :], PSF[0:1, 1024:1024 + TB], 1.0 / D,
                                None, AluOpType.mult)
                v.tensor_tensor(rrow[0:1, :], mrow[0:1, :], mrow[0:1, :],
                                AluOpType.mult)
                v.tensor_tensor(msqrow[0:1, :], msqrow[0:1, :], rrow[0:1, :],
                                AluOpType.subtract).then_inc(sVE, 1)
            v_var = E.inc("VE")
            wait("ac", "VE", v_var)

            @E.on("ac")
            def _(s):
                s.activation(msqrow[0:1, :], msqrow[0:1, :], AF.Sqrt,
                             bias=epsrow[0:1, 0:1]).then_inc(sAC, 1)
            a_sd = E.inc("AC")
            wait("ve", "AC", a_sd)

            @E.on("ve")
            def _(v):
                v.reciprocal(rrow[0:1, :], msqrow[0:1, :])
                v.tensor_tensor(mrrow[0:1, :], mrow[0:1, :], rrow[0:1, :],
                                AluOpType.mult).then_inc(sVE, 1)
            v_r = E.inc("VE")
            wait("te", "VE", v_r)

            @E.on("te")
            def _(te):
                for qb in range(2):
                    te.matmul(PSF[:, 2048 + qb * 512:2560 + qb * 512], onesrow_f[0:1, :],
                              rrow[0:1, qb * 512:qb * 512 + 512], start=True, stop=True)
                for qb in range(2):
                    i = te.matmul(PSF[:, 3072 + qb * 512:3584 + qb * 512],
                                  onesrow_f[0:1, :], mrrow[0:1, qb * 512:qb * 512 + 512],
                                  start=True, stop=True)
                i.then_inc(sMM, 1)
            mm_bc = E.inc("MM")
            wait("ac", "MM", mm_bc)

            @E.on("ac")
            def _(s):
                s.copy(rbb[:], PSF[:, 2048:2048 + TB])
                s.copy(mrbb[:], PSF[:, 3072:3072 + TB]).then_inc(sAC, 1)
            a_bc = E.inc("AC")
            u_last = 0
            for (k_, v_) in raw_waits:
                wait("ve", k_, v_)
            for fc in range(NF):
                wait("ac", "VE", u_last)

                @E.on("ac")
                def _(s, fc=fc, xbf=xbf):
                    s.copy(xfs[:], xbf[:, fc * TB:(fc + 1) * TB]).then_inc(sAC, 1)
                a_x = E.inc("AC")
                wait("ve", "AC", max(a_x, a_bc))

                @E.on("ve")
                def _(v, fc=fc):
                    v.tensor_tensor(t1f[:], xfs[:], rbb[:], AluOpType.mult)
                    v.tensor_tensor(u_v[:, fc * TB:(fc + 1) * TB], t1f[:], mrbb[:],
                                    AluOpType.subtract).then_inc(sVE, 1)
                u_last = E.inc("VE")
            return u_last, c_x

        def phase_F(src_dram, resid_dram, dst_dram, ntb, fence, raw_keys, stkey):
            w12_cons = [fence["MM"], fence["MM"]]
            w3_cons = [fence["MM"], fence["MM"]]
            xbf_guard_mm = fence["MM"]
            region_ac, region_ve = fence["AC"], fence["VE"]
            first_ve = fence["VE"]
            raw = [(k, fence.get(k, 0)) for k in raw_keys]
            st_par = [0, 0]
            for (k_, v_) in raw:
                wait("sy", k_, v_)
            for tb in range(ntb):
                tsl = slice(tb * TB, (tb + 1) * TB)
                v_u, c_x = ln_prologue(xbf_F, src_dram, tsl, xbf_guard_mm,
                                       region_ac, region_ve,
                                       raw if tb == 0 else [],
                                       first_ve if tb == 0 else 0)
                sig_acs = {}
                hid_ves = {}
                x2_mms = {}
                for hc in range(NH):
                    par = hc % 2
                    slab = w12s[par]
                    wait("sy", "MM", w12_cons[par])
                    if hc < 2:
                        wait("sy", "MM", max(w3_cons))
                    wk = "wsa" if par == 0 else "wsb"
                    c2 = dma("sy", wk, slab[:, 0:4096],
                             w12S[hc * 128:(hc + 1) * 128, :])
                    psx1 = PSF[:, 0:1024] if par == 0 else PSF[:, 2048:3072]
                    psx2 = PSF[:, 1024:2048] if par == 0 else PSF[:, 3072:4096]
                    wait("te", "wsa" if par == 0 else "wsb", c2)
                    if hc == 0:
                        wait("te", "VE", v_u)
                    if hc >= 2:
                        wait("te", "AC", sig_acs[hc - 2])
                        wait("te", "VE", hid_ves[hc - 2])

                    @E.on("te")
                    def _(te, slab=slab, psx1=psx1, psx2=psx2):
                        for half, ps in ((0, psx1), (1, psx2)):
                            for fc in range(NF):
                                for qb in range(2):
                                    i = te.matmul(
                                        ps[:, qb * 512:(qb + 1) * 512],
                                        slab[:, (half * 16 + fc) * 128:(half * 16 + fc + 1) * 128],
                                        u_v[:, fc * TB + qb * 512:fc * TB + qb * 512 + 512],
                                        start=(fc == 0), stop=(fc == NF - 1))
                            i.then_inc(sMM, 1)
                    mm_x1 = E.inc("MM")
                    mm_x2 = E.inc("MM")
                    x2_mms[hc] = mm_x2
                    w12_cons[par] = mm_x2
                    wait("ac", "MM", mm_x1)
                    if hc >= 2:
                        wait("ac", "VE", hid_ves[hc - 2])

                    @E.on("ac")
                    def _(s, par=par, psx1=psx1):
                        s.activation(sigb[:, par * TB:(par + 1) * TB], psx1,
                                     AF.Sigmoid).then_inc(sAC, 1)
                    sig_acs[hc] = E.inc("AC")
                    wait("ve", "MM", mm_x2)
                    wait("ve", "AC", sig_acs[hc])

                    @E.on("ve")
                    def _(v, par=par, psx2=psx2, hc=hc):
                        v.tensor_copy(x2b[:, par * TB:(par + 1) * TB], psx2)
                        v.tensor_tensor(hid_v[:, hc * TB:(hc + 1) * TB],
                                        sigb[:, par * TB:(par + 1) * TB],
                                        x2b[:, par * TB:(par + 1) * TB],
                                        AluOpType.mult).then_inc(sVE, 1)
                    hid_ves[hc] = E.inc("VE")

                ev_ves = {}
                for fco in range(NF):
                    par = fco % 2
                    slab = w3s[par]
                    wait("sy", "MM", w3_cons[par])
                    if fco < 2:
                        wait("sy", "MM", max(w12_cons))
                    w3k = "w3a" if par == 0 else "w3b"
                    xrk = "xra" if par == 0 else "xrb"
                    c3 = dma("sy", w3k, slab[:, 0:NH * 128],
                             w3S[fco * 128:(fco + 1) * 128, :])
                    wait("sy", "VE", ev_ves[fco - 2] if fco >= 2 else 0)
                    cr = dma("sy", xrk, xres[:, par * TB:(par + 1) * TB],
                             resid_dram[fco * 128:(fco + 1) * 128, tsl])
                    psy = PSF[:, 0:1024] if par == 0 else PSF[:, 1024:2048]
                    wait("te", w3k, c3)
                    if fco < 2:
                        wait("te", "AC", sig_acs[NH - 2 + fco])
                        wait("te", "VE", hid_ves[NH - 2 + fco])
                    else:
                        wait("te", "VE", ev_ves[fco - 2])

                    @E.on("te")
                    def _(te, slab=slab, psy=psy):
                        for hc in range(NH):
                            for qb in range(2):
                                i = te.matmul(
                                    psy[:, qb * 512:(qb + 1) * 512],
                                    slab[:, hc * 128:(hc + 1) * 128],
                                    hid_v[:, hc * TB + qb * 512:hc * TB + qb * 512 + 512],
                                    start=(hc == 0), stop=(hc == NH - 1))
                        i.then_inc(sMM, 1)
                    mm_y = E.inc("MM")
                    w3_cons[par] = mm_y
                    wait("ve", "MM", mm_y)
                    wait("ve", xrk, cr)
                    wait("ve", stkey, st_par[par])

                    @E.on("ve")
                    def _(v, par=par, psy=psy):
                        v.scalar_tensor_tensor(xbout[:, par * TB:(par + 1) * TB], psy,
                                               0.5, xres[:, par * TB:(par + 1) * TB],
                                               AluOpType.mult,
                                               AluOpType.add).then_inc(sVE, 1)
                    ev_ves[fco] = E.inc("VE")
                    wait("sy", "VE", ev_ves[fco])
                    st_par[par] = dma("sy", stkey, dst_dram[fco * 128:(fco + 1) * 128, tsl],
                                      xbout[:, par * TB:(par + 1) * TB])
                xbf_guard_mm = E.cnt["MM"]
                region_ac, region_ve = E.cnt["AC"], E.cnt["VE"]
            return E.snap()

        if upto == "q":
            def finish_q(dumps):
                for k in ("qs0", "qs1", "cs0", "cs1"):
                    wait("sy", k, E.cnt.get(k, 0))
                for (dst, src) in dumps:
                    dma("sy", "dbg", dst, src)
                v = E.cnt["dbg"]

                @E.on("sy")
                def _(e, sem=sem_for("dbg"), v=v):
                    e.wait_ge(sem, v)
            finish_q([(dbg4, gdbg[:]), (out, xB)])
            return _playback(nc, E)

        fence_f1 = phase_F(x0, x0, xA, NTB_ALL, fence_q,
                           ["qs0", "qs1", "cs0", "cs1"], "xast")

        def finish(dumps, wait_keys):
            for k in wait_keys:
                wait("sy", k, E.cnt.get(k, 0))
            for (dst, src) in dumps:
                dma("sy", "dbg", dst, src)
            v = E.cnt["dbg"]

            @E.on("sy")
            def _(e, sem=sem_for("dbg"), v=v):
                e.wait_ge(sem, v)

        if upto == "f1":
            finish([(dbg1, xA), (out, xB)], ["xast"])
            return _playback(nc, E)

        wq_cons = [fence_f1["MM"], fence_f1["MM"]]
        psq_cons_ac = [fence_f1["AC"], fence_f1["AC"]]
        psq_cons_ve = [fence_f1["VE"], fence_f1["VE"]]
        buf_free_sy = [0, 0]
        rot_evac = fence_f1["VE"]
        buf_free_ve = [0, 0]
        ro_store = [0, 0]
        xbf_guard_mm = fence_f1["MM"]
        region_ac, region_ve = fence_f1["AC"], fence_f1["VE"]
        rope_prev = 0
        wait("sy", "xast", fence_f1["xast"])
        wait("sy", "MM", fence_f1["MM"])
        for tb in range(NTB_ALL):
            tsl = slice(tb * TB, (tb + 1) * TB)
            wait("sy", "VE", rope_prev)
            c_cs = dma("sy", "acs", cos_t[:], cosb[:, tsl])
            c_sn = dma("sy", "acs", sin_t[:], sinb[:, tsl])
            v_u, c_x = ln_prologue(xbf_A, xA, tsl, xbf_guard_mm, region_ac, region_ve,
                                   [("xast", fence_f1["xast"])] if tb == 0 else [],
                                   fence_f1["VE"] if tb == 0 else 0)
            oc_list = [oc for oc in range(48) if oc >= 16 or tb < NTB_OWN]
            for idx, oc in enumerate(oc_list):
                par = idx % 2
                slab = wqs[par]
                aqk = "aq0" if par == 0 else "aq1"
                wait("sy", "MM", wq_cons[par])
                c_w = dma("sy", aqk, slab[:, 0:2048],
                          wqkvS[oc * 128:(oc + 1) * 128, :])
                psq = PSF[:, 0:1024] if par == 0 else PSF[:, 1024:2048]
                wait("te", aqk, c_w)
                if idx == 0:
                    wait("te", "VE", v_u)
                if idx >= 2:
                    wait("te", "VE", psq_cons_ac[par])

                @E.on("te")
                def _(te, slab=slab, psq=psq):
                    for fc in range(NF):
                        for qb in range(2):
                            i = te.matmul(psq[:, qb * 512:(qb + 1) * 512],
                                          slab[:, fc * 128:(fc + 1) * 128],
                                          u_v[:, fc * TB + qb * 512:fc * TB + qb * 512 + 512],
                                          start=(fc == 0), stop=(fc == NF - 1))
                    i.then_inc(sMM, 1)
                mm_q = E.inc("MM")
                wq_cons[par] = mm_q
                is_v = oc >= 32
                wait("ve", "MM", mm_q)
                wait("ve", "av0" if par == 0 else "av1", buf_free_sy[par])

                @E.on("ve")
                def _(v, par=par, psq=psq, oc=oc):
                    v.tensor_scalar(qkbf[:, par * TB:(par + 1) * TB], psq,
                                    bq_t[:, oc:oc + 1], None,
                                    AluOpType.add).then_inc(sVE, 1)
                a_e = E.inc("VE")
                psq_cons_ac[par] = a_e
                if is_v:
                    wait("sy", "VE", a_e)
                    st = dma("sy", "av0" if par == 0 else "av1",
                             vs[(oc - 32) * 128:(oc - 31) * 128, tsl],
                             qkbf[:, par * TB:(par + 1) * TB])
                    buf_free_sy[par] = st
                else:
                    wait("te", "VE", a_e)
                    wait("te", "VE", rot_evac)

                    @E.on("te")
                    def _(te, par=par):
                        for qb in range(2):
                            i = te.matmul(PSF[:, 2048 + qb * 512:2560 + qb * 512],
                                          rot_t[:],
                                          qkbf[:, par * TB + qb * 512:par * TB + qb * 512 + 512],
                                          start=True, stop=True)
                        i.then_inc(sMM, 1)
                    mm_rot = E.inc("MM")
                    wait("ve", "MM", mm_rot)
                    wait("ve", "acs", c_sn)
                    wait("ve", "ro0" if par == 0 else "ro1", ro_store[par])

                    @E.on("ve")
                    def _(v, par=par):
                        lo = slice(par * TB, (par + 1) * TB)
                        qk = qkbf[:, lo]
                        om = qkout[:, lo]
                        v.tensor_copy(rtmp2[:], PSF[:, 2048:3072])
                        v.tensor_tensor(rtmp1[:], qk, cos_t[:], AluOpType.mult)
                        v.tensor_tensor(rtmp2[:], rtmp2[:], sin_t[:], AluOpType.mult)
                        v.tensor_tensor(om, rtmp1[:], rtmp2[:],
                                        AluOpType.add).then_inc(sVE, 1)
                    v_ro = E.inc("VE")
                    rot_evac = v_ro
                    buf_free_ve[par] = v_ro
                    rope_prev = v_ro
                    wait("sy", "VE", v_ro)
                    rok = "ro0" if par == 0 else "ro1"
                    if oc < 16:
                        st = dma("sy", rok, qs[oc * 128:(oc + 1) * 128, tsl],
                                 qkout[:, par * TB:(par + 1) * TB])
                    else:
                        st = dma("sy", rok, ks[(oc - 16) * 128:(oc - 15) * 128, tsl],
                                 qkout[:, par * TB:(par + 1) * TB])
                    ro_store[par] = st
            xbf_guard_mm = E.cnt["MM"]
            region_ac, region_ve = E.cnt["AC"], E.cnt["VE"]

        fence_a = E.snap()
        if upto == "a":
            finish([(dbg1, ks), (dbg2, vs), (dbg3, qs), (out, xB)],
                   ["ro0", "ro1", "av0", "av1"])
            return _playback(nc, E)

        kh_cons = [fence_a["MM"], fence_a["MM"]]
        last_head_mm = fence_a["MM"]
        last_tr_ve = 0
        oevac_ac = 0
        rec_ve = 0
        recb_ac = 0
        osc_ve = 0
        prevq_lastexp = [0]
        for k in ("ro0", "ro1", "av0", "av1"):
            wait("sy", k, fence_a.get(k, 0))
        wait("sy", "VE", fence_a["VE"])
        for h in range(H):
            par = h % 2
            tkk = "tk0" if par == 0 else "tk1"
            wait("sy", "MM", kh_cons[par])
            c_k = dma("sy", tkk, kh_v[par], ks[h * 128:(h + 1) * 128, :])
            c_v = dma("sy", tkk, vh_v[par], vs[h * 128:(h + 1) * 128, :])
            wait("sy", "MM", last_head_mm)
            c_q = dma("sy", "tq", qh_t[:], qs[h * 128:(h + 1) * 128, :])
            tr_ves = []
            for g in range(4):
                wait("te", tkk, c_v)
                wait("te", "VE", tr_ves[-1] if tr_ves else last_tr_ve)
                if h == 0 and g == 0:
                    wait("te", "AC", fence_a["AC"])
                    wait("te", "VE", fence_a["VE"])

                @E.on("te")
                def _(te, g=g, par=par):
                    for j in range(8):
                        i = te.transpose(PSTR[:, j * 128:(j + 1) * 128],
                                         vh_v[par][:, (g * 8 + j) * 128:(g * 8 + j + 1) * 128],
                                         idn_t[:])
                    i.then_inc(sMM, 1)
                mm_tr = E.inc("MM")
                wait("ve", "MM", mm_tr)

                @E.on("ve")
                def _(v, g=g):
                    v.tensor_copy(vtm_v[:, g * 1024:(g + 1) * 1024], PSTR[:]).then_inc(sVE, 1)
                tr_ves.append(E.inc("VE"))
            last_tr_ve = tr_ves[-1]
            v_tr = tr_ves[-1]

            for Q in range(4):
                nck = 4 * Q + 4
                chunks = list(range(nck)) + [16 + s for s in range(nck)]
                ngr = len(chunks) // 2
                exp_acs = {}
                smav_mm = {}
                maskinfo = {}

                def emit_S(g):
                    c0, c1 = chunks[2 * g], chunks[2 * g + 1]
                    rot = g % 2
                    psS = PSF[:, 0:1024] if rot == 0 else PSF[:, 1024:2048]
                    wait("te", "AC", exp_acs[g - 2] if g >= 2 else prevq_lastexp[0])

                    @E.on("te")
                    def _(te, psS=psS, c0=c0, c1=c1, Q=Q, par=par):
                        for ci, ch in ((0, c0), (1, c1)):
                            i = te.matmul(psS[:, ci * 512:(ci + 1) * 512],
                                          kh_v[par][:, ch * 128:(ch + 1) * 128],
                                          qh_t[:, Q * 512:(Q + 1) * 512],
                                          start=True, stop=True)
                        i.then_inc(sMM, 1)
                    mm_s = E.inc("MM")
                    wait("ac", "MM", mm_s)
                    wait("ac", "MM", smav_mm[g - 2] if g >= 2 else 0)
                    expS = expS0 if rot == 0 else expS1

                    @E.on("ac")
                    def _(s, expS=expS, psS=psS):
                        s.activation(expS, psS, AF.Exp, scale=float(SCALE)).then_inc(sAC, 1)
                    exp_acs[g] = E.inc("AC")
                    mask_ops = []
                    for ci, ch in ((0, c0), (1, c1)):
                        band = (ch if ch < 16 else ch - 16) - 4 * Q
                        kind = "tri" if ch < 16 else "b0"
                        if 0 <= band <= 3:
                            mask_ops.append((ci, band, kind))
                            for tc in range(band):
                                mask_ops.append((ci, tc, "zero"))
                    g_ve = 0
                    if mask_ops:
                        wait("ve", "AC", exp_acs[g])

                        @E.on("ve")
                        def _(v, mask_ops=tuple(mask_ops), expS=expS):
                            for (ci, tc, kind) in mask_ops:
                                sub = expS[:, ci * 512 + tc * 128:ci * 512 + (tc + 1) * 128]
                                if kind == "tri":
                                    i = v.tensor_tensor(sub, sub, tri_t[:], AluOpType.mult)
                                elif kind == "b0":
                                    i = v.tensor_tensor(sub, sub, b0_t[:], AluOpType.mult)
                                else:
                                    i = v.tensor_copy(sub, zer_t[:])
                            i.then_inc(sVE, 1)
                        g_ve = E.inc("VE")
                    maskinfo[g] = g_ve

                def emit_AV(g):
                    c0, c1 = chunks[2 * g], chunks[2 * g + 1]
                    rot = g % 2
                    expS = expS0 if rot == 0 else expS1
                    first = g == 0
                    last = g == ngr - 1
                    if maskinfo[g]:
                        wait("te", "VE", maskinfo[g])
                    else:
                        wait("te", "AC", exp_acs[g])
                    if first:
                        wait("te", "AC", oevac_ac)
                        wait("te", "VE", rec_ve)

                    @E.on("te")
                    def _(te, expS=expS, c0=c0, c1=c1, first=first, last=last):
                        for ci in (0, 1):
                            te.matmul(PSF[0:1, 3072:3584], onescol_bf[:, 0:1],
                                      expS[:, ci * 512:(ci + 1) * 512],
                                      start=(first and ci == 0), stop=(last and ci == 1))
                        for ci, ch in ((0, c0), (1, c1)):
                            i = te.matmul(PSF[:, 2048:2560],
                                          vtm_v[:, ch * 128:(ch + 1) * 128],
                                          expS[:, ci * 512:(ci + 1) * 512],
                                          start=(first and ci == 0), stop=(last and ci == 1))
                        i.then_inc(sMM, 1)
                    smav_mm[g] = E.inc("MM")

                wait("te", tkk, c_k)
                wait("te", "tq", c_q)
                wait("te", "VE", v_tr)
                emit_S(0)
                for g in range(ngr):
                    if g + 1 < ngr:
                        emit_S(g + 1)
                    emit_AV(g)
                prevq_lastexp[0] = exp_acs[ngr - 1]
                mm_qd = smav_mm[ngr - 1]
                wait("ac", "MM", mm_qd)
                wait("ac", "VE", osc_ve)

                @E.on("ac")
                def _(s):
                    s.copy(oun_t[:], PSF[:, 2048:2560]).then_inc(sAC, 1)
                oevac_ac = E.inc("AC")
                wait("ve", "MM", mm_qd)

                @E.on("ve")
                def _(v):
                    v.reciprocal(recrow[0:1, :], PSF[0:1, 3072:3584]).then_inc(sVE, 1)
                rec_ve = E.inc("VE")
                wait("te", "VE", rec_ve)
                wait("te", "AC", recb_ac)

                @E.on("te")
                def _(te):
                    te.matmul(PSF[:, 2560:3072], onesrow_f[0:1, :], recrow[0:1, :],
                              start=True, stop=True).then_inc(sMM, 1)
                mm_rb = E.inc("MM")
                wait("ac", "MM", mm_rb)

                @E.on("ac")
                def _(s):
                    s.copy(recb_t[:], PSF[:, 2560:3072]).then_inc(sAC, 1)
                recb_ac = E.inc("AC")
                wait("ve", "AC", max(recb_ac, oevac_ac))

                @E.on("ve")
                def _(v, h=h, Q=Q):
                    v.tensor_tensor(o_sb[:, h * D + Q * 512:h * D + (Q + 1) * 512],
                                    oun_t[:], recb_t[:], AluOpType.mult).then_inc(sVE, 1)
                osc_ve = E.inc("VE")
            kh_cons[par] = E.cnt["MM"]
            last_head_mm = E.cnt["MM"]

        fence_t = E.snap()
        o_done_ve = osc_ve
        if upto == "t":
            wait("sy", "VE", o_done_ve)
            dma("sy", "dbg", dbg1, o_sb)
            dma("sy", "dbg", out, xB)
            v = E.cnt["dbg"]

            @E.on("sy")
            def _(e, sem=sem_for("dbg"), v=v):
                e.wait_ge(sem, v)
            return _playback(nc, E)

        wo_cons = [fence_t["MM"], fence_t["MM"]]
        wev = {}
        xb_st = 0
        for fco in range(NF):
            par = fco % 2
            slab = wos[par]
            wok = "wo0" if par == 0 else "wo1"
            wait("sy", "MM", wo_cons[par])
            c_w = dma("sy", wok, slab[:, 0:2048],
                      woS[fco * 128:(fco + 1) * 128, :])
            wait("sy", "VE", wev[fco - 1] if fco >= 1 else 0)
            c_r = dma("sy", "wxr", xres[:, 0:2048], xA[fco * 128:(fco + 1) * 128, 0:D])
            psy = PSF[:, 0:2048] if par == 0 else PSF[:, 2048:4096]
            wait("te", wok, c_w)
            if fco == 0:
                wait("te", "VE", o_done_ve)
                wait("te", "AC", fence_t["AC"])
            if fco >= 2:
                wait("te", "VE", wev[fco - 2])

            @E.on("te")
            def _(te, slab=slab, psy=psy):
                for fc in range(NF):
                    for qb in range(4):
                        i = te.matmul(psy[:, (qb % 4) * 512:(qb % 4) * 512 + 512],
                                      slab[:, fc * 128:(fc + 1) * 128],
                                      o_sb[:, fc * D + qb * 512:fc * D + qb * 512 + 512],
                                      start=(fc == 0), stop=(fc == NF - 1))
                i.then_inc(sMM, 1)
            mm_w = E.inc("MM")
            wo_cons[par] = mm_w
            wait("ve", "MM", mm_w)
            wait("ve", "wxr", c_r)
            wait("ve", "xbst", xb_st)

            @E.on("ve")
            def _(v, psy=psy, fco=fco):
                for half in range(4):
                    i = v.scalar_tensor_tensor(
                        xbout[:, half * 512:half * 512 + 512],
                        psy[:, half * 512:half * 512 + 512], bo_t[:, fco:fco + 1],
                        xres[:, half * 512:half * 512 + 512],
                        AluOpType.add, AluOpType.add)
                i.then_inc(sVE, 1)
            wev[fco] = E.inc("VE")
            wait("sy", "VE", wev[fco])
            xb_st = dma("sy", "xbst", xB[fco * 128:(fco + 1) * 128, :], xbout[:, 0:2048])

        fence_w = E.snap()
        if upto == "w":
            finish([(dbg1, xB), (out, xB)], ["xbst"])
            return _playback(nc, E)

        phase_F(xB, xB, out, NTB_OWN, fence_w, ["xbst"], "outst")

        v_out = E.cnt["outst"]

        @E.on("sy")
        def _(e, sem=sem_for("outst"), v=v_out):
            e.wait_ge(sem, v)

        return _playback(nc, E)


def _playback(nc, E):
    with nc.Block() as block:
        @block.sync
        def _(e):
            for fn in E.ops["sy"]:
                fn(e)

        @block.gpsimd
        def _(e):
            for fn in E.ops["gp"]:
                fn(e)

        @block.tensor
        def _(e):
            for fn in E.ops["te"]:
                fn(e)

        @block.scalar
        def _(e):
            for fn in E.ops["ac"]:
                fn(e)

        @block.vector
        def _(e):
            for fn in E.ops["ve"]:
                fn(e)
    return nc


def _rope_tables():
    p = np.arange(64, dtype=np.float32)
    theta = np.power(np.float32(10000.0),
                     (-(np.float32(2.0) ** p) / np.float32(128.0)).astype(np.float32)
                     ).astype(np.float32)
    pos = np.arange(1, T + 1, dtype=np.float32)[:, None]
    c = pos * theta[None, :]
    ang = np.concatenate([c, c], axis=1)
    return np.cos(ang).astype(np.float32), np.sin(ang).astype(np.float32)


def prep_inputs(L0_attn_x, L0_attn_Wqkv, L0_attn_bqkv, L0_attn_Wo, L0_attn_bo,
                L0_ffn_w1, L0_ffn_w2, L0_ffn_w3):
    x = np.asarray(L0_attn_x, dtype=np.float32)
    Wqkv = np.asarray(L0_attn_Wqkv, dtype=np.float32)
    bqkv = np.asarray(L0_attn_bqkv, dtype=np.float32)
    Wo = np.asarray(L0_attn_Wo, dtype=np.float32)
    bo = np.asarray(L0_attn_bo, dtype=np.float32)
    w1 = np.asarray(L0_ffn_w1, dtype=np.float32)
    w2 = np.asarray(L0_ffn_w2, dtype=np.float32)
    w3 = np.asarray(L0_ffn_w3, dtype=np.float32)

    w1T = np.ascontiguousarray(w1.T)
    w2T = np.ascontiguousarray(w2.T)
    w3T = np.ascontiguousarray(w3.T)
    wqkvT = np.ascontiguousarray(Wqkv.T)
    woT = np.ascontiguousarray(Wo.T)
    biasq_h = np.ascontiguousarray(bqkv.reshape(48, 128).T)
    biaso_h = np.ascontiguousarray(bo.reshape(16, 128).T)
    cos_f, sin_f = _rope_tables()
    tri = np.triu(np.ones((128, 128), dtype=np.float32)).astype(ml_dtypes.bfloat16)
    ident = np.eye(128, dtype=ml_dtypes.bfloat16)
    Rm = np.zeros((128, 128), dtype=np.float32)
    for dd in range(64):
        Rm[dd, dd + 64] = -1.0
        Rm[dd + 64, dd] = 1.0
    RmT = np.ascontiguousarray(Rm.T).astype(ml_dtypes.bfloat16)

    in_maps = []
    perms = []
    for c in range(8):
        b, p = c // 2, c % 2
        own = np.concatenate([np.arange(128) + (2 * m + p) * 128 for m in range(16)])
        partner = np.concatenate([np.arange(128) + (2 * m + 1 - p) * 128
                                  for m in range(16)])
        perm = np.concatenate([own, partner])
        perms.append(own)
        xc = np.ascontiguousarray(x[b][perm].T)
        cosc = np.ascontiguousarray(cos_f[perm].T).astype(ml_dtypes.bfloat16)
        sinc = np.ascontiguousarray(sin_f[perm].T).astype(ml_dtypes.bfloat16)
        b0 = np.full((128, 128), float(p), dtype=ml_dtypes.bfloat16)
        in_maps.append({
            "x0": xc, "w1T": w1T, "w2T": w2T, "w3T": w3T, "wqkvT": wqkvT,
            "woT": woT, "biasq": biasq_h, "biaso": biaso_h,
            "cosb": cosc, "sinb": sinc, "trim": tri, "b0m": b0, "idn_in": ident,
            "rot_in": RmT,
        })
    return in_maps, perms


def kernel(**inputs):
    in_maps, perms = prep_inputs(**inputs)
    nc = build_program()
    res = run_bass_kernel_spmd(nc, in_maps, core_ids=list(range(8)))

    outp = np.empty((4, T, D), dtype=np.float32)
    for c in range(8):
        o = np.asarray(res.results[c]["out"])
        outp[c // 2][perms[c]] = o.T
    return outp


# version 04 (final)
# speedup vs baseline: 1.0152x; 1.0152x over previous
import numpy as np
import ml_dtypes
import concourse.bass as bass
import concourse.mybir as mybir
from concourse.bass_utils import run_bass_kernel_spmd
from concourse.alu_op_type import AluOpType
from contextlib import ExitStack

F32 = mybir.dt.float32
BF16 = mybir.dt.bfloat16
AF = mybir.ActivationFunctionType
AX = mybir.AxisListType

D = 2048
T = 4096
NF = 16
HID = 5632
NH = 44
H = 16
TB = 1024
NTB_ALL = 4
NTB_OWN = 2
SCALE = 1.0 / np.sqrt(np.float32(128.0))
EPS_LN = 1e-5
EPS_Q = 1e-8


class Emit:
    def __init__(self):
        self.ops = {k: [] for k in ("sy", "gp", "te", "ac", "ve")}
        self.cnt = {k: 0 for k in ("MM", "AC", "VE")}

    def inc(self, sem, n=1):
        self.cnt[sem] += n
        return self.cnt[sem]

    def on(self, eng):
        def deco(fn):
            self.ops[eng].append(fn)
        return deco

    def snap(self):
        return dict(self.cnt)


_UPTO = "full"


def build_program():
    upto = _UPTO
    nc = bass.Bass("TRN2", target_bir_lowering=False, debug=False)

    x0 = nc.dram_tensor("x0", [D, T], F32, kind="ExternalInput").ap()
    w1T = nc.dram_tensor("w1T", [D, HID], F32, kind="ExternalInput").ap()
    w2T = nc.dram_tensor("w2T", [D, HID], F32, kind="ExternalInput").ap()
    w3T = nc.dram_tensor("w3T", [HID, D], F32, kind="ExternalInput").ap()
    wqkvT = nc.dram_tensor("wqkvT", [D, 3 * D], F32, kind="ExternalInput").ap()
    woT = nc.dram_tensor("woT", [D, D], F32, kind="ExternalInput").ap()
    biasq = nc.dram_tensor("biasq", [128, 48], F32, kind="ExternalInput").ap()
    biaso = nc.dram_tensor("biaso", [128, 16], F32, kind="ExternalInput").ap()
    cosb = nc.dram_tensor("cosb", [128, T], BF16, kind="ExternalInput").ap()
    sinb = nc.dram_tensor("sinb", [128, T], BF16, kind="ExternalInput").ap()
    trim = nc.dram_tensor("trim", [128, 128], BF16, kind="ExternalInput").ap()
    b0m = nc.dram_tensor("b0m", [128, 128], BF16, kind="ExternalInput").ap()
    idn_in = nc.dram_tensor("idn_in", [128, 128], BF16, kind="ExternalInput").ap()
    rot_in = nc.dram_tensor("rot_in", [128, 128], BF16, kind="ExternalInput").ap()
    out = nc.dram_tensor("out", [D, D], F32, kind="ExternalOutput").ap()
    if upto == "f1":
        dbg1 = nc.dram_tensor("dbg1", [D, T], F32, kind="ExternalOutput").ap()
    elif upto == "q":
        dbg1 = nc.dram_tensor("dbg1", [D, HID], BF16, kind="ExternalOutput").ap()
        dbg2 = nc.dram_tensor("dbg2", [D, HID], BF16, kind="ExternalOutput").ap()
        dbg3 = nc.dram_tensor("dbg3", [HID, D], BF16, kind="ExternalOutput").ap()
        dbg4 = nc.dram_tensor("dbg4", [128, NH + 2], F32, kind="ExternalOutput").ap()
    elif upto == "a":
        dbg1 = nc.dram_tensor("dbg1", [D, T], BF16, kind="ExternalOutput").ap()
        dbg2 = nc.dram_tensor("dbg2", [D, T], BF16, kind="ExternalOutput").ap()
        dbg3 = nc.dram_tensor("dbg3", [D, D], BF16, kind="ExternalOutput").ap()
    elif upto == "w":
        dbg1 = nc.dram_tensor("dbg1", [D, D], F32, kind="ExternalOutput").ap()
    elif upto == "t":
        dbg1 = nc.dram_tensor("dbg1", [128, H * D], BF16, kind="ExternalOutput").ap()

    w12S = nc.dram_tensor("w12S", [NH * 128, 2 * D], BF16).ap()
    w3S = nc.dram_tensor("w3S", [NF * 128, HID], BF16).ap()
    wqkvS = nc.dram_tensor("wqkvS", [48 * 128, D], BF16).ap()
    woS = nc.dram_tensor("woS", [NF * 128, D], BF16).ap()
    xA = nc.dram_tensor("xA", [D, T], F32).ap()
    xB = nc.dram_tensor("xB", [D, D], F32).ap()
    qs = nc.dram_tensor("qs", [D, D], BF16).ap()
    ks = nc.dram_tensor("ks", [D, T], BF16).ap()
    vs = nc.dram_tensor("vs", [D, T], BF16).ap()

    E = Emit()

    with ExitStack() as ctx:
        en = ctx.enter_context
        AR_HID = en(nc.sbuf_tensor("AR_HID", [128, 45056], BF16))
        AR_U = en(nc.sbuf_tensor("AR_U", [128, 16384], BF16))
        AR_W = en(nc.sbuf_tensor("AR_W", [128, 11264], BF16))
        onescol_bf = en(nc.sbuf_tensor("onescol_bf", [128, 1], BF16))
        onescol_f = en(nc.sbuf_tensor("onescol_f", [128, 1], F32))
        onesrow_f = en(nc.sbuf_tensor("onesrow_f", [1, 128], F32))
        epsrow = en(nc.sbuf_tensor("epsrow", [1, 1], F32))
        tri_t = en(nc.sbuf_tensor("tri_t", [128, 128], BF16))
        b0_t = en(nc.sbuf_tensor("b0_t", [128, 128], BF16))
        zer_t = en(nc.sbuf_tensor("zer_t", [128, 128], BF16))
        idn_t = en(nc.sbuf_tensor("idn_t", [128, 128], BF16))
        rot_t = en(nc.sbuf_tensor("rot_t", [128, 128], BF16))
        bq_t = en(nc.sbuf_tensor("bq_t", [128, 48], F32))
        bo_t = en(nc.sbuf_tensor("bo_t", [128, 16], F32))
        mrow = en(nc.sbuf_tensor("mrow", [1, TB], F32))
        msqrow = en(nc.sbuf_tensor("msqrow", [1, TB], F32))
        rrow = en(nc.sbuf_tensor("rrow", [1, TB], F32))
        mrrow = en(nc.sbuf_tensor("mrrow", [1, TB], F32))
        recrow = en(nc.sbuf_tensor("recrow", [1, 512], F32))
        throw = en(nc.sbuf_tensor("throw", [1, 1], F32))
        thcol = en(nc.sbuf_tensor("thcol", [128, 1], F32))
        nthcol = en(nc.sbuf_tensor("nthcol", [128, 1], F32))
        gparts = en(nc.sbuf_tensor("gparts", [128, NH], F32))
        gsum = en(nc.sbuf_tensor("gsum", [128, 1], F32))
        gdbg = en(nc.sbuf_tensor("gdbg", [128, NH + 2], F32)) if upto == "q" else None
        rbb = en(nc.sbuf_tensor("rbb", [128, TB], F32))
        mrbb = en(nc.sbuf_tensor("mrbb", [128, TB], F32))
        xsq = en(nc.sbuf_tensor("xsq", [128, 2 * TB], BF16))
        xfs = en(nc.sbuf_tensor("xfs", [128, TB], F32))
        t1f = en(nc.sbuf_tensor("t1f", [128, TB], F32))
        sigb = en(nc.sbuf_tensor("sigb", [128, 2 * TB], BF16))
        x2b = en(nc.sbuf_tensor("x2b", [128, 2 * TB], BF16))
        xres = en(nc.sbuf_tensor("xres", [128, 2 * TB], F32))
        xbout = en(nc.sbuf_tensor("xbout", [128, 2 * TB], F32))

        PSF = en(nc.psum_tensor("PSF", [128, 4096], F32))
        PSTR = PSF[:, 3584:4096].bitcast(BF16)

        sCH = en(nc.semaphore("sCH"))
        sMM = en(nc.semaphore("sMM"))
        sAC = en(nc.semaphore("sAC"))
        sVE = en(nc.semaphore("sVE"))
        SEM = {"MM": sMM, "AC": sAC, "VE": sVE}

        CH = [0]
        def chx(v, inst):
            CH[0] += 1
            inst.then_inc(sCH, 1)
            v.wait_ge(sCH, CH[0])

        def sem_for(key):
            if key not in SEM:
                SEM[key] = en(nc.semaphore("s_" + key))
                E.cnt[key] = 0
            return SEM[key]

        hid_v = AR_HID[:, 0:NH * TB]
        xbf_F = AR_HID[:, 28 * TB * 1:44 * TB]
        wtileA = AR_HID[:, 0:11264].bitcast(F32)
        wtileB = AR_HID[:, 11264:22528].bitcast(F32)
        tmpa_q = AR_HID[:, 22528:28160]
        tmpb_q = AR_HID[:, 28160:33792]
        wqtileA = AR_HID[:, 33792:39424]
        wqtileB = AR_HID[:, 39424:45056]
        xbf_A = AR_HID[:, 0:16 * TB]
        cos_t = AR_HID[:, 16 * TB:17 * TB]
        sin_t = AR_HID[:, 17 * TB:18 * TB]
        o_sb = AR_HID[:, 0:H * D]
        vtm_v = AR_HID[:, H * D:H * D + 4096]
        expS0 = AR_HID[:, H * D + 4096:H * D + 5120]
        expS1 = AR_HID[:, H * D + 5120:H * D + 6144]
        u_v = AR_U[:, 0:16 * TB]
        kh_v = [AR_U[:, 0:4096], AR_U[:, 8192:12288]]
        vh_v = [AR_U[:, 4096:8192], AR_U[:, 12288:16384]]
        castA = AR_U[:, 0:6144]
        castB = AR_U[:, 6144:12288]
        w12s = [AR_W[:, 0:4096], AR_W[:, 4096:8192]]
        w3s = [AR_W[:, 0:5632], AR_W[:, 5632:11264]]
        wqs = [AR_W[:, 0:2048], AR_W[:, 2048:4096]]
        wos = [AR_W[:, 0:2048], AR_W[:, 2048:4096]]
        qkbf = AR_W[:, 4096:6144]
        qkout = AR_W[:, 6144:8192]
        rtmp1 = AR_W[:, 8192:9216]
        rtmp2 = AR_W[:, 9216:10240]
        qh_t = AR_W[:, 4096:6144]
        oun_t = AR_W[:, 6144:6656]
        recb_t = AR_W[:, 6656:7168]

        def dma(q, key, out_ap, in_ap):
            sem = sem_for(key)
            v = E.inc(key, 16)

            @E.on(q)
            def _(e, out_ap=out_ap, in_ap=in_ap, sem=sem):
                e.dma_start(out=out_ap, in_=in_ap).then_inc(sem, 16)
            return v

        def wait(eng, sem_key, val):
            if val is None or val <= 0:
                return
            sem = sem_for(sem_key)

            @E.on(eng)
            def _(e, s=sem, v=val):
                e.wait_ge(s, v)

        def chunk_cols(src2d, c0, w):
            return src2d[:, c0:c0 + w].rearrange("(a p) c -> p a c", p=128)

        def row_tile(src2d, t, C):
            return src2d[t * 128:(t + 1) * 128, :].rearrange("(a p) c -> p a c", p=128)

        def sb3(view, w):
            return view.rearrange("p (a c) -> p a c", c=w)

        def one3(view, w):
            return view[:, 0:w].rearrange("p (a c) -> p a c", c=w)

        dma("sy", "ini", tri_t[:], trim)
        dma("sy", "ini", b0_t[:], b0m)
        dma("sy", "ini", idn_t[:], idn_in)
        dma("sy", "ini", rot_t[:], rot_in)
        dma("sy", "ini", bq_t[:], biasq)
        c_bo = dma("sy", "ini", bo_t[:], biaso)
        wait("ve", "ini", c_bo)

        @E.on("ve")
        def _(v):
            v.memset(onescol_bf[:], 1.0)
            v.memset(onescol_f[:], 1.0)
            v.memset(onesrow_f[:], 1.0)
            v.memset(epsrow[:], EPS_LN)
            v.memset(zer_t[:], 0.0)
            v.memset(gparts[:], 0.0).then_inc(sVE, 1)
        vInit = E.inc("VE")

        prev_ac = 0
        prev_ve = vInit
        def qdest(which, t):
            if which == 0:
                return w12S[:, t * 128:(t + 1) * 128].rearrange(
                    "(hc p) c -> p hc c", p=128)
            if which == 1:
                return w12S[:, 2048 + t * 128:2048 + (t + 1) * 128].rearrange(
                    "(hc p) c -> p hc c", p=128)
            return w3S[:, t * 128:(t + 1) * 128].rearrange(
                "(fo p) c -> p fo c", p=128)

        for wi, (wsrc, ntile, CW) in enumerate(((w1T, NF, HID), (w2T, NF, HID),
                                                (w3T, NH, D))):
            tile_ve = []
            for t in range(ntile):
                buf = wtileA if t % 2 == 0 else wtileB
                gk = "qg0" if t % 2 == 0 else "qg1"
                wait("sy", "VE", tile_ve[t - 2] if t >= 2 else prev_ve)
                c_ld = dma("sy", gk, one3(buf, CW), row_tile(wsrc, t, CW))
                wait("ve", gk, c_ld)

                @E.on("ve")
                def _(v, buf=buf, t=t, CW=CW):
                    v.reduce_sum(gparts[:, t:t + 1], buf[:, 0:CW], axis=AX.X,
                                 apply_absolute_value=True).then_inc(sVE, 1)
                tile_ve.append(E.inc("VE"))

            @E.on("ve")
            def _(v, nt=ntile, w_=tile_ve[-1]):
                v.wait_ge(sVE, w_)
                v.reduce_sum(gsum[:, 0:1], gparts[:, 0:nt], axis=AX.X).then_inc(sVE, 1)
            vred = E.inc("VE")
            if upto == "q" and wsrc is w1T:
                @E.on("ve")
                def _(v):
                    v.tensor_copy(gdbg[:, 0:NH], gparts[:])
                    v.tensor_copy(gdbg[:, NH:NH + 1], gsum[:]).then_inc(sVE, 1)
                E.inc("VE")
            wait("te", "VE", vred)
            wait("te", "AC", prev_ac)

            @E.on("te")
            def _(te):
                te.matmul(PSF[0:1, 0:1], gsum[:, 0:1], onescol_f[:, 0:1],
                          start=True, stop=True).then_inc(sMM, 1)
            mm_g = E.inc("MM")
            wait("ve", "MM", mm_g)

            @E.on("ve")
            def _(v, scale=0.5 / (128.0 * ntile * CW)):
                v.tensor_scalar(throw[0:1, 0:1], PSF[0:1, 0:1], scale, 0.5 * EPS_Q,
                                AluOpType.mult, AluOpType.add).then_inc(sVE, 1)
            vth = E.inc("VE")
            wait("te", "VE", vth)

            @E.on("te")
            def _(te):
                te.matmul(PSF[:, 512:513], onesrow_f[0:1, :], throw[0:1, 0:1],
                          start=True, stop=True).then_inc(sMM, 1)
            mm_th = E.inc("MM")
            wait("ac", "MM", mm_th)

            @E.on("ac")
            def _(s):
                s.copy(thcol[:], PSF[:, 512:513]).then_inc(sAC, 1)
            prev_ac = E.inc("AC")
            wait("ve", "AC", prev_ac)

            @E.on("ve")
            def _(v):
                v.tensor_scalar(nthcol[:], thcol[:], -1.0, None,
                                AluOpType.mult).then_inc(sVE, 1)
            vneg = E.inc("VE")

            q_ve = []
            q_st = []
            for t in range(ntile):
                buf = wtileA if t % 2 == 0 else wtileB
                qbuf = wqtileA if t % 2 == 0 else wqtileB
                qk_ = "qq0" if t % 2 == 0 else "qq1"
                sk_ = "qs0" if t % 2 == 0 else "qs1"
                wait("sy", "VE", q_ve[t - 2] if t >= 2 else vneg)
                c_ld = dma("sy", qk_, one3(buf, CW), row_tile(wsrc, t, CW))
                wait("ve", qk_, c_ld)
                if t >= 2:
                    wait("ve", sk_, q_st[t - 2])

                @E.on("ve")
                def _(v, buf=buf, qbuf=qbuf, CW=CW):
                    v.tensor_scalar(tmpa_q[:, 0:CW], buf[:, 0:CW], thcol[:, 0:1], None,
                                    AluOpType.is_gt)
                    chx(v, v.tensor_scalar(tmpb_q[:, 0:CW], buf[:, 0:CW], nthcol[:, 0:1],
                                           None, AluOpType.is_lt))
                    v.tensor_tensor(qbuf[:, 0:CW], tmpa_q[:, 0:CW], tmpb_q[:, 0:CW],
                                    AluOpType.subtract).then_inc(sVE, 1)
                q_ve.append(E.inc("VE"))
                wait("sy", "VE", q_ve[-1])
                q_st.append(dma("sy", sk_, qdest(wi, t),
                                qbuf[:, 0:CW].rearrange("p (a c) -> p a c", c=128)))
            prev_ve = q_ve[-1]

        cast_st = []
        for (wsrc, wdstS, CW, ntile) in ((wqkvT, wqkvS, 3 * D, NF), (woT, woS, D, NF)):
            for t in range(ntile):
                i2 = len(cast_st)
                lk = "cl0" if i2 % 2 == 0 else "cl1"
                sk = "cs0" if i2 % 2 == 0 else "cs1"
                buf = castA if i2 % 2 == 0 else castB
                if i2 >= 2:
                    wait("gp", sk, cast_st[-2][1])
                c_ld = dma("gp", lk, one3(buf, CW), row_tile(wsrc, t, CW))
                wait("sy", lk, c_ld)
                dstap = wdstS[:, t * 128:(t + 1) * 128].rearrange(
                    "(oc p) c -> p oc c", p=128)
                cast_st.append((sk, dma("sy", sk, dstap,
                                        buf[:, 0:CW].rearrange("p (a c) -> p a c", c=128))))

        fence_q = E.snap()

        def ln_prologue(xbf, src_dram, tsl, xbf_guard_mm, region_ac, region_ve,
                        raw_waits, prev_ve_fence):
            wait("gp", "MM", xbf_guard_mm)
            for (k_, v_) in raw_waits:
                wait("gp", k_, v_)
            wait("gp", "VE", prev_ve_fence)
            c_x = dma("gp", "gpx", sb3(xbf, TB), src_dram[:, tsl].rearrange("(a p) t -> p a t", p=128))
            wait("te", "gpx", c_x)
            wait("te", "AC", region_ac)
            wait("te", "VE", region_ve)
            sq_acs = []
            sq_mms = []
            for fc in range(NF):
                sqb = xsq[:, (fc % 2) * TB:(fc % 2 + 1) * TB]
                wait("ac", "gpx", c_x)
                if fc >= 2:
                    wait("ac", "MM", sq_mms[fc - 2])
                if fc == 0:
                    wait("ac", "MM", xbf_guard_mm)

                @E.on("ac")
                def _(s, sqb=sqb, fc=fc, xbf=xbf):
                    s.activation(sqb, xbf[:, fc * TB:(fc + 1) * TB],
                                 AF.Square).then_inc(sAC, 1)
                sq_acs.append(E.inc("AC"))
                wait("te", "AC", sq_acs[-1])

                @E.on("te")
                def _(te, fc=fc, sqb=sqb, xbf=xbf):
                    for qb in range(2):
                        te.matmul(PSF[0:1, qb * 512:(qb + 1) * 512], onescol_bf[:, 0:1],
                                  xbf[:, fc * TB + qb * 512:fc * TB + qb * 512 + 512],
                                  start=(fc == 0), stop=(fc == NF - 1))
                    for qb in range(2):
                        i = te.matmul(PSF[0:1, 1024 + qb * 512:1536 + qb * 512],
                                      onescol_bf[:, 0:1], sqb[:, qb * 512:qb * 512 + 512],
                                      start=(fc == 0), stop=(fc == NF - 1))
                    i.then_inc(sMM, 1)
                sq_mms.append(E.inc("MM"))
            mm_stats = sq_mms[-1]
            wait("ve", "MM", mm_stats)

            @E.on("ve")
            def _(v):
                v.tensor_scalar(mrow[0:1, :], PSF[0:1, 0:TB], 1.0 / D, None,
                                AluOpType.mult)
                chx(v, v.tensor_scalar(msqrow[0:1, :], PSF[0:1, 1024:1024 + TB],
                                       1.0 / D, None, AluOpType.mult))
                chx(v, v.tensor_tensor(rrow[0:1, :], mrow[0:1, :], mrow[0:1, :],
                                       AluOpType.mult))
                v.tensor_tensor(msqrow[0:1, :], msqrow[0:1, :], rrow[0:1, :],
                                AluOpType.subtract).then_inc(sVE, 1)
            v_var = E.inc("VE")
            wait("ac", "VE", v_var)

            @E.on("ac")
            def _(s):
                s.activation(msqrow[0:1, :], msqrow[0:1, :], AF.Sqrt,
                             bias=epsrow[0:1, 0:1]).then_inc(sAC, 1)
            a_sd = E.inc("AC")
            wait("ve", "AC", a_sd)

            @E.on("ve")
            def _(v):
                chx(v, v.reciprocal(rrow[0:1, :], msqrow[0:1, :]))
                v.tensor_tensor(mrrow[0:1, :], mrow[0:1, :], rrow[0:1, :],
                                AluOpType.mult).then_inc(sVE, 1)
            v_r = E.inc("VE")
            wait("te", "VE", v_r)

            @E.on("te")
            def _(te):
                for qb in range(2):
                    te.matmul(PSF[:, 2048 + qb * 512:2560 + qb * 512], onesrow_f[0:1, :],
                              rrow[0:1, qb * 512:qb * 512 + 512], start=True, stop=True)
                for qb in range(2):
                    i = te.matmul(PSF[:, 3072 + qb * 512:3584 + qb * 512],
                                  onesrow_f[0:1, :], mrrow[0:1, qb * 512:qb * 512 + 512],
                                  start=True, stop=True)
                i.then_inc(sMM, 1)
            mm_bc = E.inc("MM")
            wait("ac", "MM", mm_bc)

            @E.on("ac")
            def _(s):
                s.copy(rbb[:], PSF[:, 2048:2048 + TB])
                s.copy(mrbb[:], PSF[:, 3072:3072 + TB]).then_inc(sAC, 1)
            a_bc = E.inc("AC")
            u_last = 0
            for (k_, v_) in raw_waits:
                wait("ve", k_, v_)
            for fc in range(NF):
                wait("ac", "VE", u_last)

                @E.on("ac")
                def _(s, fc=fc, xbf=xbf):
                    s.copy(xfs[:], xbf[:, fc * TB:(fc + 1) * TB]).then_inc(sAC, 1)
                a_x = E.inc("AC")
                wait("ve", "AC", max(a_x, a_bc))

                @E.on("ve")
                def _(v, fc=fc):
                    chx(v, v.tensor_tensor(t1f[:], xfs[:], rbb[:], AluOpType.mult))
                    v.tensor_tensor(u_v[:, fc * TB:(fc + 1) * TB], t1f[:], mrbb[:],
                                    AluOpType.subtract).then_inc(sVE, 1)
                u_last = E.inc("VE")
            return u_last, c_x

        def phase_F(src_dram, resid_dram, dst_dram, ntb, fence, raw_keys, stkey):
            w12_cons = [fence["MM"], fence["MM"]]
            w3_cons = [fence["MM"], fence["MM"]]
            xbf_guard_mm = fence["MM"]
            region_ac, region_ve = fence["AC"], fence["VE"]
            first_ve = fence["VE"]
            raw = [(k, fence.get(k, 0)) for k in raw_keys]
            st_par = [0, 0]
            for (k_, v_) in raw:
                wait("sy", k_, v_)
            for tb in range(ntb):
                tsl = slice(tb * TB, (tb + 1) * TB)
                v_u, c_x = ln_prologue(xbf_F, src_dram, tsl, xbf_guard_mm,
                                       region_ac, region_ve,
                                       raw if tb == 0 else [],
                                       first_ve if tb == 0 else 0)
                sig_acs = {}
                hid_ves = {}
                x2_mms = {}
                for hc in range(NH):
                    par = hc % 2
                    slab = w12s[par]
                    wait("sy", "MM", w12_cons[par])
                    if hc < 2:
                        wait("sy", "MM", max(w3_cons))
                    wk = "wsa" if par == 0 else "wsb"
                    c2 = dma("sy", wk, slab[:, 0:4096],
                             w12S[hc * 128:(hc + 1) * 128, :])
                    psx1 = PSF[:, 0:1024] if par == 0 else PSF[:, 2048:3072]
                    psx2 = PSF[:, 1024:2048] if par == 0 else PSF[:, 3072:4096]
                    wait("te", "wsa" if par == 0 else "wsb", c2)
                    if hc == 0:
                        wait("te", "VE", v_u)
                    if hc >= 2:
                        wait("te", "AC", sig_acs[hc - 2])
                        wait("te", "VE", hid_ves[hc - 2])

                    @E.on("te")
                    def _(te, slab=slab, psx1=psx1, psx2=psx2):
                        for half, ps in ((0, psx1), (1, psx2)):
                            for fc in range(NF):
                                for qb in range(2):
                                    i = te.matmul(
                                        ps[:, qb * 512:(qb + 1) * 512],
                                        slab[:, (half * 16 + fc) * 128:(half * 16 + fc + 1) * 128],
                                        u_v[:, fc * TB + qb * 512:fc * TB + qb * 512 + 512],
                                        start=(fc == 0), stop=(fc == NF - 1))
                            i.then_inc(sMM, 1)
                    mm_x1 = E.inc("MM")
                    mm_x2 = E.inc("MM")
                    x2_mms[hc] = mm_x2
                    w12_cons[par] = mm_x2
                    wait("ac", "MM", mm_x1)
                    if hc >= 2:
                        wait("ac", "VE", hid_ves[hc - 2])

                    @E.on("ac")
                    def _(s, par=par, psx1=psx1):
                        s.activation(sigb[:, par * TB:(par + 1) * TB], psx1,
                                     AF.Sigmoid).then_inc(sAC, 1)
                    sig_acs[hc] = E.inc("AC")
                    wait("ve", "MM", mm_x2)
                    wait("ve", "AC", sig_acs[hc])

                    @E.on("ve")
                    def _(v, par=par, psx2=psx2, hc=hc):
                        chx(v, v.tensor_copy(x2b[:, par * TB:(par + 1) * TB], psx2))
                        v.tensor_tensor(hid_v[:, hc * TB:(hc + 1) * TB],
                                        sigb[:, par * TB:(par + 1) * TB],
                                        x2b[:, par * TB:(par + 1) * TB],
                                        AluOpType.mult).then_inc(sVE, 1)
                    hid_ves[hc] = E.inc("VE")

                ev_ves = {}
                for fco in range(NF):
                    par = fco % 2
                    slab = w3s[par]
                    wait("sy", "MM", w3_cons[par])
                    if fco < 2:
                        wait("sy", "MM", max(w12_cons))
                    w3k = "w3a" if par == 0 else "w3b"
                    xrk = "xra" if par == 0 else "xrb"
                    c3 = dma("sy", w3k, slab[:, 0:NH * 128],
                             w3S[fco * 128:(fco + 1) * 128, :])
                    wait("sy", "VE", ev_ves[fco - 2] if fco >= 2 else 0)
                    cr = dma("sy", xrk, xres[:, par * TB:(par + 1) * TB],
                             resid_dram[fco * 128:(fco + 1) * 128, tsl])
                    psy = PSF[:, 0:1024] if par == 0 else PSF[:, 1024:2048]
                    wait("te", w3k, c3)
                    if fco < 2:
                        wait("te", "AC", sig_acs[NH - 2 + fco])
                        wait("te", "VE", hid_ves[NH - 2 + fco])
                    else:
                        wait("te", "VE", ev_ves[fco - 2])

                    @E.on("te")
                    def _(te, slab=slab, psy=psy):
                        for hc in range(NH):
                            for qb in range(2):
                                i = te.matmul(
                                    psy[:, qb * 512:(qb + 1) * 512],
                                    slab[:, hc * 128:(hc + 1) * 128],
                                    hid_v[:, hc * TB + qb * 512:hc * TB + qb * 512 + 512],
                                    start=(hc == 0), stop=(hc == NH - 1))
                        i.then_inc(sMM, 1)
                    mm_y = E.inc("MM")
                    w3_cons[par] = mm_y
                    wait("ve", "MM", mm_y)
                    wait("ve", xrk, cr)
                    wait("ve", stkey, st_par[par])

                    @E.on("ve")
                    def _(v, par=par, psy=psy):
                        v.scalar_tensor_tensor(xbout[:, par * TB:(par + 1) * TB], psy,
                                               0.5, xres[:, par * TB:(par + 1) * TB],
                                               AluOpType.mult,
                                               AluOpType.add).then_inc(sVE, 1)
                    ev_ves[fco] = E.inc("VE")
                    wait("sy", "VE", ev_ves[fco])
                    st_par[par] = dma("sy", stkey, dst_dram[fco * 128:(fco + 1) * 128, tsl],
                                      xbout[:, par * TB:(par + 1) * TB])
                xbf_guard_mm = E.cnt["MM"]
                region_ac, region_ve = E.cnt["AC"], E.cnt["VE"]
            return E.snap()

        if upto == "q":
            def finish_q(dumps):
                for k in ("qs0", "qs1", "cs0", "cs1"):
                    wait("sy", k, E.cnt.get(k, 0))
                for (dst, src) in dumps:
                    dma("sy", "dbg", dst, src)
                v = E.cnt["dbg"]

                @E.on("sy")
                def _(e, sem=sem_for("dbg"), v=v):
                    e.wait_ge(sem, v)
            finish_q([(dbg4, gdbg[:]), (out, xB)])
            return _playback(nc, E)

        fence_f1 = phase_F(x0, x0, xA, NTB_ALL, fence_q,
                           ["qs0", "qs1", "cs0", "cs1"], "xast")

        def finish(dumps, wait_keys):
            for k in wait_keys:
                wait("sy", k, E.cnt.get(k, 0))
            for (dst, src) in dumps:
                dma("sy", "dbg", dst, src)
            v = E.cnt["dbg"]

            @E.on("sy")
            def _(e, sem=sem_for("dbg"), v=v):
                e.wait_ge(sem, v)

        if upto == "f1":
            finish([(dbg1, xA), (out, xB)], ["xast"])
            return _playback(nc, E)

        wq_cons = [fence_f1["MM"], fence_f1["MM"]]
        psq_cons_ac = [fence_f1["AC"], fence_f1["AC"]]
        psq_cons_ve = [fence_f1["VE"], fence_f1["VE"]]
        buf_free_sy = [0, 0]
        rot_evac = fence_f1["VE"]
        buf_free_ve = [0, 0]
        ro_store = [0, 0]
        xbf_guard_mm = fence_f1["MM"]
        region_ac, region_ve = fence_f1["AC"], fence_f1["VE"]
        rope_prev = 0
        wait("sy", "xast", fence_f1["xast"])
        wait("sy", "MM", fence_f1["MM"])
        for tb in range(NTB_ALL):
            tsl = slice(tb * TB, (tb + 1) * TB)
            wait("sy", "VE", rope_prev)
            c_cs = dma("sy", "acs", cos_t[:], cosb[:, tsl])
            c_sn = dma("sy", "acs", sin_t[:], sinb[:, tsl])
            v_u, c_x = ln_prologue(xbf_A, xA, tsl, xbf_guard_mm, region_ac, region_ve,
                                   [("xast", fence_f1["xast"])] if tb == 0 else [],
                                   fence_f1["VE"] if tb == 0 else 0)
            oc_list = [oc for oc in range(48) if oc >= 16 or tb < NTB_OWN]
            for idx, oc in enumerate(oc_list):
                par = idx % 2
                slab = wqs[par]
                aqk = "aq0" if par == 0 else "aq1"
                wait("sy", "MM", wq_cons[par])
                c_w = dma("sy", aqk, slab[:, 0:2048],
                          wqkvS[oc * 128:(oc + 1) * 128, :])
                psq = PSF[:, 0:1024] if par == 0 else PSF[:, 1024:2048]
                wait("te", aqk, c_w)
                if idx == 0:
                    wait("te", "VE", v_u)
                if idx >= 2:
                    wait("te", "VE", psq_cons_ac[par])

                @E.on("te")
                def _(te, slab=slab, psq=psq):
                    for fc in range(NF):
                        for qb in range(2):
                            i = te.matmul(psq[:, qb * 512:(qb + 1) * 512],
                                          slab[:, fc * 128:(fc + 1) * 128],
                                          u_v[:, fc * TB + qb * 512:fc * TB + qb * 512 + 512],
                                          start=(fc == 0), stop=(fc == NF - 1))
                    i.then_inc(sMM, 1)
                mm_q = E.inc("MM")
                wq_cons[par] = mm_q
                is_v = oc >= 32
                wait("ve", "MM", mm_q)
                wait("ve", "av0" if par == 0 else "av1", buf_free_sy[par])

                @E.on("ve")
                def _(v, par=par, psq=psq, oc=oc):
                    v.tensor_scalar(qkbf[:, par * TB:(par + 1) * TB], psq,
                                    bq_t[:, oc:oc + 1], None,
                                    AluOpType.add).then_inc(sVE, 1)
                a_e = E.inc("VE")
                psq_cons_ac[par] = a_e
                if is_v:
                    wait("sy", "VE", a_e)
                    st = dma("sy", "av0" if par == 0 else "av1",
                             vs[(oc - 32) * 128:(oc - 31) * 128, tsl],
                             qkbf[:, par * TB:(par + 1) * TB])
                    buf_free_sy[par] = st
                else:
                    wait("te", "VE", a_e)
                    wait("te", "VE", rot_evac)

                    @E.on("te")
                    def _(te, par=par):
                        for qb in range(2):
                            i = te.matmul(PSF[:, 2048 + qb * 512:2560 + qb * 512],
                                          rot_t[:],
                                          qkbf[:, par * TB + qb * 512:par * TB + qb * 512 + 512],
                                          start=True, stop=True)
                        i.then_inc(sMM, 1)
                    mm_rot = E.inc("MM")
                    wait("ve", "MM", mm_rot)
                    wait("ve", "acs", c_sn)
                    wait("ve", "ro0" if par == 0 else "ro1", ro_store[par])

                    @E.on("ve")
                    def _(v, par=par):
                        lo = slice(par * TB, (par + 1) * TB)
                        qk = qkbf[:, lo]
                        om = qkout[:, lo]
                        v.tensor_copy(rtmp2[:], PSF[:, 2048:3072])
                        chx(v, v.tensor_tensor(rtmp1[:], qk, cos_t[:], AluOpType.mult))
                        chx(v, v.tensor_tensor(rtmp2[:], rtmp2[:], sin_t[:],
                                               AluOpType.mult))
                        v.tensor_tensor(om, rtmp1[:], rtmp2[:],
                                        AluOpType.add).then_inc(sVE, 1)
                    v_ro = E.inc("VE")
                    rot_evac = v_ro
                    buf_free_ve[par] = v_ro
                    rope_prev = v_ro
                    wait("sy", "VE", v_ro)
                    rok = "ro0" if par == 0 else "ro1"
                    if oc < 16:
                        st = dma("sy", rok, qs[oc * 128:(oc + 1) * 128, tsl],
                                 qkout[:, par * TB:(par + 1) * TB])
                    else:
                        st = dma("sy", rok, ks[(oc - 16) * 128:(oc - 15) * 128, tsl],
                                 qkout[:, par * TB:(par + 1) * TB])
                    ro_store[par] = st
            xbf_guard_mm = E.cnt["MM"]
            region_ac, region_ve = E.cnt["AC"], E.cnt["VE"]

        fence_a = E.snap()
        if upto == "a":
            finish([(dbg1, ks), (dbg2, vs), (dbg3, qs), (out, xB)],
                   ["ro0", "ro1", "av0", "av1"])
            return _playback(nc, E)

        kh_cons = [fence_a["MM"], fence_a["MM"]]
        last_head_mm = fence_a["MM"]
        last_tr_ve = 0
        oevac_ac = 0
        rec_ve = 0
        recb_ac = 0
        osc_ve = 0
        prevq_lastexp = [0]
        for k in ("ro0", "ro1", "av0", "av1"):
            wait("sy", k, fence_a.get(k, 0))
        wait("sy", "VE", fence_a["VE"])
        for h in range(H):
            par = h % 2
            tkk = "tk0" if par == 0 else "tk1"
            wait("sy", "MM", kh_cons[par])
            c_k = dma("sy", tkk, kh_v[par], ks[h * 128:(h + 1) * 128, :])
            c_v = dma("sy", tkk, vh_v[par], vs[h * 128:(h + 1) * 128, :])
            wait("sy", "MM", last_head_mm)
            c_q = dma("sy", "tq", qh_t[:], qs[h * 128:(h + 1) * 128, :])
            tr_ves = []
            for g in range(4):
                wait("te", tkk, c_v)
                wait("te", "VE", tr_ves[-1] if tr_ves else last_tr_ve)
                if h == 0 and g == 0:
                    wait("te", "AC", fence_a["AC"])
                    wait("te", "VE", fence_a["VE"])

                @E.on("te")
                def _(te, g=g, par=par):
                    for j in range(8):
                        i = te.transpose(PSTR[:, j * 128:(j + 1) * 128],
                                         vh_v[par][:, (g * 8 + j) * 128:(g * 8 + j + 1) * 128],
                                         idn_t[:])
                    i.then_inc(sMM, 1)
                mm_tr = E.inc("MM")
                wait("ve", "MM", mm_tr)

                @E.on("ve")
                def _(v, g=g):
                    v.tensor_copy(vtm_v[:, g * 1024:(g + 1) * 1024], PSTR[:]).then_inc(sVE, 1)
                tr_ves.append(E.inc("VE"))
            last_tr_ve = tr_ves[-1]
            v_tr = tr_ves[-1]

            for Q in range(4):
                nck = 4 * Q + 4
                chunks = list(range(nck)) + [16 + s for s in range(nck)]
                ngr = len(chunks) // 2
                exp_acs = {}
                smav_mm = {}
                maskinfo = {}

                def emit_S(g):
                    c0, c1 = chunks[2 * g], chunks[2 * g + 1]
                    rot = g % 2
                    psS = PSF[:, 0:1024] if rot == 0 else PSF[:, 1024:2048]
                    wait("te", "AC", exp_acs[g - 2] if g >= 2 else prevq_lastexp[0])

                    @E.on("te")
                    def _(te, psS=psS, c0=c0, c1=c1, Q=Q, par=par):
                        for ci, ch in ((0, c0), (1, c1)):
                            i = te.matmul(psS[:, ci * 512:(ci + 1) * 512],
                                          kh_v[par][:, ch * 128:(ch + 1) * 128],
                                          qh_t[:, Q * 512:(Q + 1) * 512],
                                          start=True, stop=True)
                        i.then_inc(sMM, 1)
                    mm_s = E.inc("MM")
                    wait("ac", "MM", mm_s)
                    wait("ac", "MM", smav_mm[g - 2] if g >= 2 else 0)
                    expS = expS0 if rot == 0 else expS1

                    @E.on("ac")
                    def _(s, expS=expS, psS=psS):
                        s.activation(expS, psS, AF.Exp, scale=float(SCALE)).then_inc(sAC, 1)
                    exp_acs[g] = E.inc("AC")
                    mask_ops = []
                    for ci, ch in ((0, c0), (1, c1)):
                        band = (ch if ch < 16 else ch - 16) - 4 * Q
                        kind = "tri" if ch < 16 else "b0"
                        if 0 <= band <= 3:
                            mask_ops.append((ci, band, kind))
                            for tc in range(band):
                                mask_ops.append((ci, tc, "zero"))
                    g_ve = 0
                    if mask_ops:
                        wait("ve", "AC", exp_acs[g])

                        @E.on("ve")
                        def _(v, mask_ops=tuple(mask_ops), expS=expS):
                            for (ci, tc, kind) in mask_ops:
                                sub = expS[:, ci * 512 + tc * 128:ci * 512 + (tc + 1) * 128]
                                if kind == "tri":
                                    i = v.tensor_tensor(sub, sub, tri_t[:], AluOpType.mult)
                                elif kind == "b0":
                                    i = v.tensor_tensor(sub, sub, b0_t[:], AluOpType.mult)
                                else:
                                    i = v.tensor_copy(sub, zer_t[:])
                            i.then_inc(sVE, 1)
                        g_ve = E.inc("VE")
                    maskinfo[g] = g_ve

                def emit_AV(g):
                    c0, c1 = chunks[2 * g], chunks[2 * g + 1]
                    rot = g % 2
                    expS = expS0 if rot == 0 else expS1
                    first = g == 0
                    last = g == ngr - 1
                    if maskinfo[g]:
                        wait("te", "VE", maskinfo[g])
                    else:
                        wait("te", "AC", exp_acs[g])
                    if first:
                        wait("te", "AC", oevac_ac)
                        wait("te", "VE", rec_ve)

                    @E.on("te")
                    def _(te, expS=expS, c0=c0, c1=c1, first=first, last=last):
                        for ci in (0, 1):
                            te.matmul(PSF[0:1, 3072:3584], onescol_bf[:, 0:1],
                                      expS[:, ci * 512:(ci + 1) * 512],
                                      start=(first and ci == 0), stop=(last and ci == 1))
                        for ci, ch in ((0, c0), (1, c1)):
                            i = te.matmul(PSF[:, 2048:2560],
                                          vtm_v[:, ch * 128:(ch + 1) * 128],
                                          expS[:, ci * 512:(ci + 1) * 512],
                                          start=(first and ci == 0), stop=(last and ci == 1))
                        i.then_inc(sMM, 1)
                    smav_mm[g] = E.inc("MM")

                wait("te", tkk, c_k)
                wait("te", "tq", c_q)
                wait("te", "VE", v_tr)
                emit_S(0)
                for g in range(ngr):
                    if g + 1 < ngr:
                        emit_S(g + 1)
                    emit_AV(g)
                prevq_lastexp[0] = exp_acs[ngr - 1]
                mm_qd = smav_mm[ngr - 1]
                wait("ac", "MM", mm_qd)
                wait("ac", "VE", osc_ve)

                @E.on("ac")
                def _(s):
                    s.copy(oun_t[:], PSF[:, 2048:2560]).then_inc(sAC, 1)
                oevac_ac = E.inc("AC")
                wait("ve", "MM", mm_qd)

                @E.on("ve")
                def _(v):
                    v.reciprocal(recrow[0:1, :], PSF[0:1, 3072:3584]).then_inc(sVE, 1)
                rec_ve = E.inc("VE")
                wait("te", "VE", rec_ve)
                wait("te", "AC", recb_ac)

                @E.on("te")
                def _(te):
                    te.matmul(PSF[:, 2560:3072], onesrow_f[0:1, :], recrow[0:1, :],
                              start=True, stop=True).then_inc(sMM, 1)
                mm_rb = E.inc("MM")
                wait("ac", "MM", mm_rb)

                @E.on("ac")
                def _(s):
                    s.copy(recb_t[:], PSF[:, 2560:3072]).then_inc(sAC, 1)
                recb_ac = E.inc("AC")
                wait("ve", "AC", max(recb_ac, oevac_ac))

                @E.on("ve")
                def _(v, h=h, Q=Q):
                    v.tensor_tensor(o_sb[:, h * D + Q * 512:h * D + (Q + 1) * 512],
                                    oun_t[:], recb_t[:], AluOpType.mult).then_inc(sVE, 1)
                osc_ve = E.inc("VE")
            kh_cons[par] = E.cnt["MM"]
            last_head_mm = E.cnt["MM"]

        fence_t = E.snap()
        o_done_ve = osc_ve
        if upto == "t":
            wait("sy", "VE", o_done_ve)
            dma("sy", "dbg", dbg1, o_sb)
            dma("sy", "dbg", out, xB)
            v = E.cnt["dbg"]

            @E.on("sy")
            def _(e, sem=sem_for("dbg"), v=v):
                e.wait_ge(sem, v)
            return _playback(nc, E)

        wo_cons = [fence_t["MM"], fence_t["MM"]]
        wev = {}
        xb_st = 0
        for fco in range(NF):
            par = fco % 2
            slab = wos[par]
            wok = "wo0" if par == 0 else "wo1"
            wait("sy", "MM", wo_cons[par])
            c_w = dma("sy", wok, slab[:, 0:2048],
                      woS[fco * 128:(fco + 1) * 128, :])
            wait("sy", "VE", wev[fco - 1] if fco >= 1 else 0)
            c_r = dma("sy", "wxr", xres[:, 0:2048], xA[fco * 128:(fco + 1) * 128, 0:D])
            psy = PSF[:, 0:2048] if par == 0 else PSF[:, 2048:4096]
            wait("te", wok, c_w)
            if fco == 0:
                wait("te", "VE", o_done_ve)
                wait("te", "AC", fence_t["AC"])
            if fco >= 2:
                wait("te", "VE", wev[fco - 2])

            @E.on("te")
            def _(te, slab=slab, psy=psy):
                for fc in range(NF):
                    for qb in range(4):
                        i = te.matmul(psy[:, (qb % 4) * 512:(qb % 4) * 512 + 512],
                                      slab[:, fc * 128:(fc + 1) * 128],
                                      o_sb[:, fc * D + qb * 512:fc * D + qb * 512 + 512],
                                      start=(fc == 0), stop=(fc == NF - 1))
                i.then_inc(sMM, 1)
            mm_w = E.inc("MM")
            wo_cons[par] = mm_w
            wait("ve", "MM", mm_w)
            wait("ve", "wxr", c_r)
            wait("ve", "xbst", xb_st)

            @E.on("ve")
            def _(v, psy=psy, fco=fco):
                for half in range(4):
                    i = v.scalar_tensor_tensor(
                        xbout[:, half * 512:half * 512 + 512],
                        psy[:, half * 512:half * 512 + 512], bo_t[:, fco:fco + 1],
                        xres[:, half * 512:half * 512 + 512],
                        AluOpType.add, AluOpType.add)
                i.then_inc(sVE, 1)
            wev[fco] = E.inc("VE")
            wait("sy", "VE", wev[fco])
            xb_st = dma("sy", "xbst", xB[fco * 128:(fco + 1) * 128, :], xbout[:, 0:2048])

        fence_w = E.snap()
        if upto == "w":
            finish([(dbg1, xB), (out, xB)], ["xbst"])
            return _playback(nc, E)

        phase_F(xB, xB, out, NTB_OWN, fence_w, ["xbst"], "outst")

        v_out = E.cnt["outst"]

        @E.on("sy")
        def _(e, sem=sem_for("outst"), v=v_out):
            e.wait_ge(sem, v)

        return _playback(nc, E)


def _playback(nc, E):
    with nc.Block() as block:
        @block.sync
        def _(e):
            for fn in E.ops["sy"]:
                fn(e)

        @block.gpsimd
        def _(e):
            for fn in E.ops["gp"]:
                fn(e)

        @block.tensor
        def _(e):
            for fn in E.ops["te"]:
                fn(e)

        @block.scalar
        def _(e):
            for fn in E.ops["ac"]:
                fn(e)

        @block.vector
        def _(e):
            for fn in E.ops["ve"]:
                fn(e)
    return nc


def _rope_tables():
    p = np.arange(64, dtype=np.float32)
    theta = np.power(np.float32(10000.0),
                     (-(np.float32(2.0) ** p) / np.float32(128.0)).astype(np.float32)
                     ).astype(np.float32)
    pos = np.arange(1, T + 1, dtype=np.float32)[:, None]
    c = pos * theta[None, :]
    ang = np.concatenate([c, c], axis=1)
    return np.cos(ang).astype(np.float32), np.sin(ang).astype(np.float32)


def prep_inputs(L0_attn_x, L0_attn_Wqkv, L0_attn_bqkv, L0_attn_Wo, L0_attn_bo,
                L0_ffn_w1, L0_ffn_w2, L0_ffn_w3):
    x = np.asarray(L0_attn_x, dtype=np.float32)
    Wqkv = np.asarray(L0_attn_Wqkv, dtype=np.float32)
    bqkv = np.asarray(L0_attn_bqkv, dtype=np.float32)
    Wo = np.asarray(L0_attn_Wo, dtype=np.float32)
    bo = np.asarray(L0_attn_bo, dtype=np.float32)
    w1 = np.asarray(L0_ffn_w1, dtype=np.float32)
    w2 = np.asarray(L0_ffn_w2, dtype=np.float32)
    w3 = np.asarray(L0_ffn_w3, dtype=np.float32)

    w1T = np.ascontiguousarray(w1.T)
    w2T = np.ascontiguousarray(w2.T)
    w3T = np.ascontiguousarray(w3.T)
    wqkvT = np.ascontiguousarray(Wqkv.T)
    woT = np.ascontiguousarray(Wo.T)
    biasq_h = np.ascontiguousarray(bqkv.reshape(48, 128).T)
    biaso_h = np.ascontiguousarray(bo.reshape(16, 128).T)
    cos_f, sin_f = _rope_tables()
    tri = np.triu(np.ones((128, 128), dtype=np.float32)).astype(ml_dtypes.bfloat16)
    ident = np.eye(128, dtype=ml_dtypes.bfloat16)
    Rm = np.zeros((128, 128), dtype=np.float32)
    for dd in range(64):
        Rm[dd, dd + 64] = -1.0
        Rm[dd + 64, dd] = 1.0
    RmT = np.ascontiguousarray(Rm.T).astype(ml_dtypes.bfloat16)

    in_maps = []
    perms = []
    for c in range(8):
        b, p = c // 2, c % 2
        own = np.concatenate([np.arange(128) + (2 * m + p) * 128 for m in range(16)])
        partner = np.concatenate([np.arange(128) + (2 * m + 1 - p) * 128
                                  for m in range(16)])
        perm = np.concatenate([own, partner])
        perms.append(own)
        xc = np.ascontiguousarray(x[b][perm].T)
        cosc = np.ascontiguousarray(cos_f[perm].T).astype(ml_dtypes.bfloat16)
        sinc = np.ascontiguousarray(sin_f[perm].T).astype(ml_dtypes.bfloat16)
        b0 = np.full((128, 128), float(p), dtype=ml_dtypes.bfloat16)
        in_maps.append({
            "x0": xc, "w1T": w1T, "w2T": w2T, "w3T": w3T, "wqkvT": wqkvT,
            "woT": woT, "biasq": biasq_h, "biaso": biaso_h,
            "cosb": cosc, "sinb": sinc, "trim": tri, "b0m": b0, "idn_in": ident,
            "rot_in": RmT,
        })
    return in_maps, perms


def kernel(**inputs):
    in_maps, perms = prep_inputs(**inputs)
    nc = build_program()
    res = run_bass_kernel_spmd(nc, in_maps, core_ids=list(range(8)))

    outp = np.empty((4, T, D), dtype=np.float32)
    for c in range(8):
        o = np.asarray(res.results[c]["out"])
        outp[c // 2][perms[c]] = o.T
    return outp
